# version 03
# baseline (speedup 1.0000x reference)
import numpy as np
import ml_dtypes
from contextlib import ExitStack

import concourse.bass as bass
import concourse.mybir as mybir
from concourse.bass_utils import run_bass_kernel_spmd

F32 = mybir.dt.float32
BF16 = mybir.dt.bfloat16
AF = mybir.ActivationFunctionType
ALU = mybir.AluOpType

B_CORE = 2
T = 2048
D = 2048
N = 16
HID = 8192
TT = 512
NTT = (B_CORE * T) // TT
NSUB = TT // 128
NDC = D // 128
NJC = HID // 128
EPS = 1e-5


class Sem:
    def __init__(self, stack, nc, name):
        self.h = stack.enter_context(nc.semaphore(name))
        self.name = name
        self.n = 0


def build_nc(ntt=NTT, dbg_phase=99):
    nc = bass.Bass("TRN2", target_bir_lowering=False)

    x_d = nc.dram_tensor("x", [B_CORE * T, D], F32, kind="ExternalInput").ap()
    wx_d = nc.dram_tensor("wx", [128, NDC * 160], F32, kind="ExternalInput").ap()
    wdt_d = nc.dram_tensor("wdt", [128, D], F32, kind="ExternalInput").ap()
    alog_d = nc.dram_tensor("alog", [128, NDC * N], F32, kind="ExternalInput").ap()
    bdt_d = nc.dram_tensor("bdt", [128, NDC], F32, kind="ExternalInput").ap()
    dpar_d = nc.dram_tensor("dpar", [128, NDC], F32, kind="ExternalInput").ap()
    usc_d = nc.dram_tensor("usc", [128, NJC], F32, kind="ExternalInput").ap()
    dsc_d = nc.dram_tensor("dsc", [128, NDC], F32, kind="ExternalInput").ap()
    osc_d = nc.dram_tensor("osc", [128, 1], F32, kind="ExternalInput").ap()
    upw_d = nc.dram_tensor("upw", [NJC, 128, 2048], F32, kind="ExternalInput").ap()
    dnw_d = nc.dram_tensor("dnw", [32, 128, 4096], F32, kind="ExternalInput").ap()
    idf_d = nc.dram_tensor("identf", [128, 128], F32, kind="ExternalInput").ap()
    idb_d = nc.dram_tensor("identb", [128, 128], BF16, kind="ExternalInput").ap()
    sel0_d = nc.dram_tensor("sel0", [32, 2048], BF16, kind="ExternalInput").ap()
    sel1_d = nc.dram_tensor("sel1", [32, 2048], BF16, kind="ExternalInput").ap()
    cv_d = nc.dram_tensor("cvals", [128, 3], F32, kind="ExternalInput").ap()
    out_d = nc.dram_tensor("out", [B_CORE * T, D], F32, kind="ExternalOutput").ap()
    scr_up = nc.dram_tensor("scr_up", [NJC, 128, 2048], BF16, kind="Internal").ap()
    scr_dn = nc.dram_tensor("scr_dn", [32, 128, 4096], BF16, kind="Internal").ap()

    stk = ExitStack()
    sb = lambda name, shape, dt: stk.enter_context(nc.sbuf_tensor(name, shape, dt))
    pst = lambda name, shape, dt: stk.enter_context(nc.psum_tensor(name, shape, dt))
    with stk:
        xn_d = sb("xn_d", [128, NDC * TT], BF16)
        y_sb = sb("y", [128, NDC * TT], F32)
        x_t = sb("x_t", [128, 2 * D], F32)
        xnt = sb("xnt", [128, D], BF16)
        junk = xnt
        ln8 = sb("ln8", [128, 8], F32)
        bct = sb("bct", [128, 16 * TT], BF16)
        dr_sb = sb("dr", [128, TT], BF16)
        bc_sb = sb("bc", [32, TT], BF16)
        sel_sb = [sb("sel0s", [32, 2048], BF16), sb("sel1s", [32, 2048], BF16)]
        delta_t = sb("delta", [128, 2 * TT], F32)
        u_t = sb("u", [128, 2 * TT], F32)
        dA_t = sb("dA", [128, 2 * TT], F32)
        Bu_t = sb("Bu", [128, 2 * TT], F32)
        h_t = sb("h", [128, 2 * TT], F32)
        w_t = sb("w", [128, 2 * TT], F32)
        ws_t = sb("ws", [128, 2 * TT], F32)
        y2_sb = sb("y2", [128, 2 * TT], F32)
        h0 = sb("h0", [128, NDC * N], F32)
        hid = sb("hid", [128, 32 * TT], BF16)
        wup = sb("wup", [128, 2 * 2048], BF16)
        wdn = sb("wdn", [128, 2 * 2048], BF16)
        out_t = sb("out_t", [128, 2 * 1024], F32)
        wx_sb = sb("wx_sb", [128, NDC * 160], BF16)
        wdt_sb = sb("wdt_sb", [128, D], BF16)
        negA = sb("negA", [128, NDC * N], F32)
        bdt_sb = sb("bdt_sb", [128, NDC], F32)
        dpar_sb = sb("dpar_sb", [128, NDC], F32)
        usc_sb = sb("usc_sb", [128, NJC], F32)
        dsos = sb("dsos", [128, NDC], F32)
        osc_sb = sb("osc_sb", [128, 1], F32)
        cv_sb = sb("cv_sb", [128, 3], F32)
        idf_sb = sb("idf_sb", [128, 128], F32)
        idb_sb = sb("idb_sb", [128, 128], BF16)
        stg = sb("stg", [128, 2048], F32)
        sgn = sb("sgn", [128, 2 * 1024], BF16)

        ps_hid = pst("ps_hid", [128, 1024], F32)
        ps_dn = pst("ps_dn", [128, 1024], F32)
        ps_at = pst("ps_at", [128, 1024], F32)
        ps_trb = pst("ps_trb", [128, 2048], BF16)

        s_cst = Sem(stk, nc, "s_cst")
        s_pl = [Sem(stk, nc, "s_pl0"), Sem(stk, nc, "s_pl1")]
        s_sw = [Sem(stk, nc, "s_sw0"), Sem(stk, nc, "s_sw1")]
        s_x = [Sem(stk, nc, "s_x0"), Sem(stk, nc, "s_x1")]
        s_wu = [Sem(stk, nc, "s_wu0"), Sem(stk, nc, "s_wu1")]
        s_wd = [Sem(stk, nc, "s_wd0"), Sem(stk, nc, "s_wd1")]
        s_st = [Sem(stk, nc, "s_st0"), Sem(stk, nc, "s_st1")]
        s_t = Sem(stk, nc, "s_t")
        s_upg = Sem(stk, nc, "s_upg")
        s_dng = Sem(stk, nc, "s_dng")
        s_s = Sem(stk, nc, "s_s")
        s_rle = Sem(stk, nc, "s_rle")
        s_v = Sem(stk, nc, "s_v")
        s_p = Sem(stk, nc, "s_p")
        s_dne = Sem(stk, nc, "s_dne")

        m = {}
        W = {}

        def wait(eng, ename, s, v):
            if not v or v <= 0:
                return
            key = (ename, s.name)
            if W.get(key, 0) >= v:
                return
            eng.wait_ge(s.h, v)
            W[key] = v

        def inc(instr, s, k=1):
            instr.then_inc(s.h, k)
            s.n += k
            return instr

        TE, SC, VE, PO, SY = nc.tensor, nc.scalar, nc.vector, nc.gpsimd, nc.sync

        inc(SY.dma_start(stg[:, 0:1280], wx_d[:, 0:1280]), s_cst, 16)
        inc(SY.dma_start(stg[:, 1280:2048], wx_d[:, 1280:2048]), s_cst, 16)
        inc(SY.dma_start(negA[:, :], alog_d[:, :]), s_cst, 16)
        inc(SY.dma_start(bdt_sb[:, :], bdt_d[:, :]), s_cst, 16)
        inc(SY.dma_start(dpar_sb[:, :], dpar_d[:, :]), s_cst, 16)
        inc(SY.dma_start(usc_sb[:, :], usc_d[:, :]), s_cst, 16)
        inc(SY.dma_start(dsos[:, :], dsc_d[:, :]), s_cst, 16)
        inc(SY.dma_start(osc_sb[:, :], osc_d[:, :]), s_cst, 16)
        inc(SY.dma_start(cv_sb[:, :], cv_d[:, :]), s_cst, 16)
        inc(SY.dma_start(idf_sb[:, :], idf_d[:, :]), s_cst, 16)
        inc(SY.dma_start(idb_sb[:, :], idb_d[:, :]), s_cst, 16)
        inc(SY.dma_start(sel_sb[0][:, :], sel0_d[:, :]), s_cst, 16)
        inc(SY.dma_start(sel_sb[1][:, :], sel1_d[:, :]), s_cst, 16)
        m["constld"] = s_cst.n

        if dbg_phase != 20:
            SC.add_instruction(mybir.InstLoadActFuncSet(
                name=nc.get_next_instruction_name(), act_func_set_id=6, ins=[], outs=[]))

        if dbg_phase == 24:
            m["wxcast0"] = s_s.n
            m["wxcast"] = s_s.n
            m["wdtcast"] = s_s.n
            m["negAdone"] = s_s.n
            m["dsosdone"] = s_v.n
        if dbg_phase != 24:
            wait(SC, "s", s_cst, m["constld"])
            inc(SC.copy(wx_sb[:, 0:1280], stg[:, 0:1280]), s_s)
        if dbg_phase != 24:
            inc(SC.copy(wx_sb[:, 1280:2048], stg[:, 1280:2048]), s_s); m["wxcast0"] = s_s.n
            inc(SC.activation(negA[:, :], negA[:, :], AF.Exp), s_s); m["negAe"] = s_s.n
            wait(SC, "s", s_s, m["negAe"])
            inc(SC.mul(negA[:, :], negA[:, :], cv_sb[:, 2:3]), s_s); m["negAdone"] = s_s.n
            wait(VE, "v", s_cst, m["constld"])
            inc(VE.tensor_scalar(dsos[:, :], dsos[:, :], osc_sb[:, 0:1], None, ALU.mult), s_v)
            m["dsosdone"] = s_v.n
            wait(SY, "sy", s_s, m["wxcast0"])
            inc(SY.dma_start(stg[:, 0:512], wx_d[:, 2048:2560]), s_cst, 16); m["wxld2"] = s_cst.n
            wait(SC, "s", s_cst, m["wxld2"])
            inc(SC.copy(wx_sb[:, 2048:2560], stg[:, 0:512]), s_s); m["wxcast"] = s_s.n
            wait(SY, "sy", s_s, m["wxcast"])
            inc(SY.dma_start(stg[:, 0:2048], wdt_d[:, :]), s_cst, 16); m["wdtld"] = s_cst.n
            wait(SC, "s", s_cst, m["wdtld"])
            inc(SC.copy(wdt_sb[:, :], stg[:, 0:2048]), s_s); m["wdtcast"] = s_s.n

        PREP_TOTAL = 256
        prep = {"i": PREP_TOTAL if dbg_phase == 25 else 0}
        pm_sign = {}

        def prep_src(p):
            if p < 64:
                t_, c = p // 2, p % 2
                return (upw_d[t_, :, c * 1024:(c + 1) * 1024],
                        scr_up[t_, :, c * 1024:(c + 1) * 1024])
            if p < 128:
                j = p - 64
                t_, c = j // 4, j % 4
                return (dnw_d[t_, :, c * 1024:(c + 1) * 1024],
                        scr_dn[t_, :, c * 1024:(c + 1) * 1024])
            if p < 192:
                j = p - 128
                t_, c = 32 + j // 2, j % 2
                return (upw_d[t_, :, c * 1024:(c + 1) * 1024],
                        scr_up[t_, :, c * 1024:(c + 1) * 1024])
            j = p - 192
            t_, c = 16 + j // 4, j % 4
            return (dnw_d[t_, :, c * 1024:(c + 1) * 1024],
                    scr_dn[t_, :, c * 1024:(c + 1) * 1024])

        def prep_pump(nchunks):
            if dbg_phase == 22:
                return
            for _ in range(nchunks):
                i = prep["i"]
                if i >= PREP_TOTAL:
                    return
                prep["i"] = i + 1
                src, dst = prep_src(i)
                slot = i % 2
                wait(SY, "sy", s_s, pm_sign.get(i - 2, 0))
                if i < 2:
                    wait(SY, "sy", s_s, m["wdtcast"])
                inc(SY.dma_start(stg[:, slot * 1024:(slot + 1) * 1024], src),
                    s_pl[slot], 16)
                wait(SC, "s", s_pl[slot], 16 * (i // 2 + 1))
                wait(SC, "s", s_sw[slot], 16 * (i // 2))
                inc(SC.sign(sgn[:, slot * 1024:(slot + 1) * 1024],
                            stg[:, slot * 1024:(slot + 1) * 1024]), s_s)
                pm_sign[i] = s_s.n
                wait(SY, "sy", s_s, pm_sign[i])
                inc(SY.dma_start(dst, sgn[:, slot * 1024:(slot + 1) * 1024]),
                    s_sw[slot], 16)

        def scr_up_ready(eng, ename, jcg):
            v = 16 * ((jcg + 1) if jcg < 32 else (64 + (jcg - 32) + 1))
            wait(eng, ename, s_sw[0], v)
            wait(eng, ename, s_sw[1], v)

        def scr_dn_ready(eng, ename, tile, c):
            if tile < 16:
                v = 16 * (32 + 2 * tile + c + 1)
            else:
                v = 16 * (96 + 2 * (tile - 16) + c + 1)
            wait(eng, ename, s_sw[0], v)
            wait(eng, ename, s_sw[1], v)

        def prep_pump_until(n):
            while prep["i"] < n:
                prep_pump(1)

        xcnt = [0, 0]
        stcnt = [0, 0]

        for k in range(ntt):
            row0 = k * TT

            def emit_xload(tsub):
                slot = tsub % 2
                if tsub < 2:
                    wait(SY, "sy", s_v, m.get(("epiadd", k - 1, tsub + 2, 3), 0))
                else:
                    wait(SY, "sy", s_s, m[("xa", k * 4 + tsub - 2)])
                inc(SY.dma_start(x_t[:, slot * D:(slot + 1) * D],
                                 x_d[row0 + tsub * 128: row0 + (tsub + 1) * 128, :]),
                    s_x[slot], 16)
                xcnt[slot] += 1
                m[("xld", k, tsub)] = (slot, 16 * xcnt[slot])

            emit_xload(0)
            emit_xload(1)

            for tsub in range(NSUB):
                lnseq = k * 4 + tsub
                slot = tsub % 2
                xs = x_t[:, slot * D:(slot + 1) * D]
                xsl, xv = m[("xld", k, tsub)]
                if dbg_phase in (21, 22, 24, 25, 31):
                    wait(SC, "s", s_x[xsl], xv)
                    wait(SC, "s", s_t, m.get(("xnT", lnseq - 1), 0))
                    wait(SC, "s", s_s, m.get(("xa", lnseq - 1), 0))
                    inc(SC.copy(xnt[:, :], xs), s_s)
                    m[("xa", lnseq)] = s_s.n
                else:
                    wait(SC, "s", s_x[xsl], xv)
                if dbg_phase in (21, 22, 24, 25, 31):
                    pass
                else:
                    wait(SC, "s", s_v, m.get(("lnv", lnseq - 1), 0))
                    wait(SC, "s", s_t, m.get(("xnT", lnseq - 1), 0))
                    wait(SC, "s", s_s, m.get(("xa", lnseq - 1), 0))
                    inc(SC.activation(junk[:, :], xs, AF.Square, accum_out=ln8[:, 0:1]), s_s)
                    m[("lnacc0", lnseq)] = s_s.n
                    wait(SC, "s", s_s, m[("lnacc0", lnseq)])
                    inc(SC.activation(junk[:, :], xs, AF.Copy, accum_out=ln8[:, 1:2]), s_s)
                    m[("lnacc", lnseq)] = s_s.n
                    wait(VE, "v", s_s, m[("lnacc", lnseq)])
                    wait(VE, "v", s_s, m.get(("nmr", lnseq - 1), 0))
                    inc(VE.tensor_scalar(ln8[:, 2:3], ln8[:, 1:2], -1.0 / D, None, ALU.mult), s_v)
                    inc(VE.tensor_scalar(ln8[:, 3:4], ln8[:, 0:1], 1.0 / D, None, ALU.mult), s_v)
                    m[("lnv1", lnseq)] = s_v.n
                    wait(SC, "s", s_v, m[("lnv1", lnseq)])
                    inc(SC.activation(ln8[:, 4:5], ln8[:, 2:3], AF.Square), s_s)
                    m[("m2", lnseq)] = s_s.n
                    wait(VE, "v", s_s, m[("m2", lnseq)])
                    wait(VE, "v", s_v, m[("lnv1", lnseq)])
                    inc(VE.tensor_tensor(ln8[:, 3:4], ln8[:, 3:4], ln8[:, 4:5], ALU.subtract), s_v)
                    m[("lnv", lnseq)] = s_v.n
                    wait(SC, "s", s_v, m[("lnv", lnseq)])
                    inc(SC.activation(ln8[:, 5:6], ln8[:, 3:4], AF.Ln, bias=cv_sb[:, 0:1]), s_s)
                    m[("lnc1", lnseq)] = s_s.n
                    wait(SC, "s", s_s, m[("lnc1", lnseq)])
                    inc(SC.activation(ln8[:, 6:7], ln8[:, 5:6], AF.Exp, scale=cv_sb[:, 1:2]), s_s)
                    m[("lnc2", lnseq)] = s_s.n
                    wait(SC, "s", s_s, m[("lnc2", lnseq)])
                    inc(SC.activation(ln8[:, 7:8], ln8[:, 2:3], AF.Identity, scale=ln8[:, 6:7]), s_s)
                    m[("nmr", lnseq)] = s_s.n
                    wait(SC, "s", s_s, m[("nmr", lnseq)])
                    inc(SC.activation(xnt[:, :], xs, AF.Identity,
                                      scale=ln8[:, 6:7], bias=ln8[:, 7:8]), s_s)
                    m[("xa", lnseq)] = s_s.n
                if tsub + 2 < NSUB:
                    emit_xload(tsub + 2)

                for g in range(4 if dbg_phase >= 12 else 0):
                    pslot = g % 2
                    gseq = lnseq * 4 + g
                    wait(TE, "t", s_s, m[("xa", lnseq)])
                    wait(TE, "t", s_s, m.get(("xTe", gseq - 2), 0))
                    wait(TE, "t", s_cst, m["constld"])
                    for i in range(4):
                        dk = g * 4 + i
                        instr = TE.transpose(
                            ps_trb[:, pslot * 1024 + i * 128: pslot * 1024 + (i + 1) * 128],
                            xnt[:, dk * 128:(dk + 1) * 128], idb_sb[:, :])
                    inc(instr, s_t)
                    m[("xT", gseq)] = s_t.n
                    if dbg_phase in (12, 14):
                        continue
                    if dbg_phase == 15 and (tsub, g) != (0, 0):
                        continue
                    if dbg_phase == 16 and tsub != 0:
                        continue
                    if dbg_phase in (17, 20, 21, 22, 24, 25):
                        if (tsub, g) == (0, 0):
                            wait(SC, "s", s_t, m[("xT", gseq)])
                            inc(SC.copy(xn_d[:, 0:512],
                                        ps_trb[:, pslot * 512:(pslot + 1) * 512]), s_s)
                        continue
                    if dbg_phase == 18:
                        if (tsub, g) == (0, 0):
                            wait(SC, "s", s_t, m[("xT", gseq)])
                            inc(VE.tensor_copy(
                                xn_d[:, :].rearrange("p (dk t) -> p dk t", dk=NDC)[
                                    :, 0:4, 0:128],
                                ps_trb[:, 0:512].rearrange("p (a b) -> p a b", a=4)), s_v)
                        continue
                    wait(SC, "s", s_t, m[("xT", gseq)])
                    if k > 0:
                        wait(SC, "s", s_upg, k * NJC)
                        wait(SC, "s", s_v, m.get(("u", (k * 2 - 1) * NDC + 15), 0))
                    src_ap = ps_trb[:, pslot * 1024: pslot * 1024 + 512].rearrange(
                        "p (a b) -> p a b", a=4)
                    dst_ap = xn_d[:, :].rearrange("p (dk t) -> p dk t", dk=NDC)[
                        :, g * 4:(g + 1) * 4, tsub * 128:(tsub + 1) * 128]
                    inc(SC.copy(dst_ap, src_ap), s_s)
                    m[("xTe", gseq)] = s_s.n
                m[("xnT", lnseq)] = s_t.n
            m[("xTe_all", k)] = s_s.n

            prep_pump(16)
            if dbg_phase <= 1 or dbg_phase in (13, 14, 15, 16, 17, 18, 20, 21, 22, 24, 25, 31):
                continue

            wait(TE, "t", s_s, m[("xTe_all", k)])
            wait(TE, "t", s_s, m["wxcast"])
            wait(TE, "t", s_s, m.get(("spall", k * 2 - 1), 0))
            for dk in range(NDC):
                instr = TE.matmul(ps_at[:, 0:512],
                                  wx_sb[:, dk * 160: dk * 160 + 128],
                                  xn_d[:, dk * TT:(dk + 1) * TT],
                                  start=(dk == 0), stop=(dk == NDC - 1))
            inc(instr, s_t); m[("xz", k)] = s_t.n
            for dk in range(NDC):
                instr = TE.matmul(ps_at[0:32, 512:1024],
                                  wx_sb[:, dk * 160 + 128: dk * 160 + 160],
                                  xn_d[:, dk * TT:(dk + 1) * TT],
                                  start=(dk == 0), stop=(dk == NDC - 1))
            inc(instr, s_t); m[("bc", k)] = s_t.n
            wait(SC, "s", s_t, m[("xz", k)])
            wait(SC, "s", s_t, m.get(("dmm", (k * 2 - 1) * NDC + 15), 0))
            inc(SC.copy(dr_sb[:, :], ps_at[:, 0:512]), s_s)
            wait(SC, "s", s_t, m[("bc", k)])
            wait(SC, "s", s_t, m.get(("bcm", k * 2 - 1, 15), 0))
            inc(SC.copy(bc_sb[:, :], ps_at[0:32, 512:1024]), s_s)
            m[("xze", k)] = s_s.n
            if dbg_phase <= 2:
                continue

            for nh in range(1 if dbg_phase == 5 else 2):
                pidx = k * 2 + nh
                for i in range(16):
                    pslot = i % 2
                    wait(TE, "t", s_s, m[("xze", k)])
                    wait(TE, "t", s_s, m.get(("bce", pidx, i - 2), 0))
                    wait(TE, "t", s_s, m.get(("spall", pidx - 1), 0))
                    wait(TE, "t", s_p, m.get(("poolp", pidx - 1), 0))
                    wait(TE, "t", s_v, m.get(("multp", pidx - 1), 0))
                    inc(TE.matmul(ps_at[:, pslot * 512:(pslot + 1) * 512],
                                  sel_sb[nh][:, i * 128:(i + 1) * 128],
                                  bc_sb[:, :], start=True, stop=True), s_t)
                    m[("bcm", pidx, i)] = s_t.n
                    wait(SC, "s", s_t, m[("bcm", pidx, i)])
                    wait(SC, "s", s_p, m.get(("poolp", pidx - 1), 0))
                    wait(SC, "s", s_v, m.get(("multp", pidx - 1), 0))
                    inc(SC.copy(bct[:, i * 512:(i + 1) * 512],
                                ps_at[:, pslot * 512:(pslot + 1) * 512]), s_s)
                    m[("bce", pidx, i)] = s_s.n

                if k % 4 == 0 and nh == 0:
                    for _c in range(NDC * N):
                        wait(VE, "v", s_v, m.get(("h0m", _c), 0))
                    inc(VE.memset(h0[:, :], 0.0), s_v)
                    for _c in range(NDC * N):
                        m[("h0m", _c)] = s_v.n

                for dc in range(NDC):
                    useq = pidx * NDC + dc
                    pslot = dc % 2
                    wait(TE, "t", s_s, m[("xze", k)])
                    wait(TE, "t", s_s, m[("bce", pidx, 15)])
                    wait(TE, "t", s_s, m.get(("sp", useq - 2), 0))
                    wait(TE, "t", s_s, m["wdtcast"])
                    inc(TE.matmul(ps_at[:, pslot * 512:(pslot + 1) * 512],
                                  wdt_sb[:, dc * 128:(dc + 1) * 128],
                                  dr_sb[:, :], start=True, stop=True), s_t)
                    m[("dmm", useq)] = s_t.n
                    wait(SC, "s", s_t, m[("dmm", useq)])
                    wait(SC, "s", s_v, m.get(("u", useq - 2), 0))
                    dslot = delta_t[:, pslot * TT:(pslot + 1) * TT]
                    inc(SC.activation(dslot, ps_at[:, pslot * 512:(pslot + 1) * 512],
                                      AF.Exp, bias=bdt_sb[:, dc:dc + 1]), s_s)
                    m[("spe", useq)] = s_s.n
                    wait(SC, "s", s_s, m[("spe", useq)])
                    inc(SC.activation(dslot, dslot, AF.Ln, bias=1.0), s_s)
                    m[("sp", useq)] = s_s.n
                    if nh == 0:
                        wait(SC, "s", s_t, m.get(("epiT", k - 1), 0))
                        wait(SC, "s", s_s, m[("xTe_all", k)])
                        inc(SC.activation(y_sb[:, dc * TT:(dc + 1) * TT],
                                          xn_d[:, dc * TT:(dc + 1) * TT],
                                          AF.Identity, scale=dpar_sb[:, dc:dc + 1]), s_s)
                        m[("yi", k, dc)] = s_s.n
                    wait(VE, "v", s_s, m[("sp", useq)])
                    wait(VE, "v", s_p, m.get(("Bu", (useq - 2) * 8 + 7), 0))
                    inc(VE.tensor_tensor(u_t[:, pslot * TT:(pslot + 1) * TT],
                                         dslot, xn_d[:, dc * TT:(dc + 1) * TT],
                                         ALU.mult), s_v)
                    m[("u", useq)] = s_v.n

                    for nl in range(8):
                        gseq = useq * 8 + nl
                        ncol = dc * N + nh * 8 + nl
                        aslot = nl % 2
                        wait(SC, "s", s_v, m.get(("scan", gseq - 2), 0))
                        wait(SC, "s", s_s, m[("sp", useq)])
                        wait(SC, "s", s_s, m["negAdone"])
                        inc(SC.activation(dA_t[:, aslot * TT:(aslot + 1) * TT],
                                          dslot, AF.Exp,
                                          scale=negA[:, ncol:ncol + 1]), s_s)
                        m[("dA", gseq)] = s_s.n
                        wait(PO, "p", s_v, m[("u", useq)])
                        wait(PO, "p", s_s, m[("bce", pidx, nl)])
                        wait(PO, "p", s_v, m.get(("scan", gseq - 2), 0))
                        inc(PO.tensor_tensor(Bu_t[:, aslot * TT:(aslot + 1) * TT],
                                             u_t[:, pslot * TT:(pslot + 1) * TT],
                                             bct[:, nl * TT:(nl + 1) * TT],
                                             ALU.mult), s_p)
                        m[("Bu", gseq)] = s_p.n
                        wait(VE, "v", s_s, m[("dA", gseq)])
                        wait(VE, "v", s_p, m[("Bu", gseq)])
                        wait(VE, "v", s_v, m.get(("h0m", ncol), 0))
                        inc(VE.tensor_tensor_scan(
                            h_t[:, aslot * TT:(aslot + 1) * TT],
                            dA_t[:, aslot * TT:(aslot + 1) * TT],
                            Bu_t[:, aslot * TT:(aslot + 1) * TT],
                            h0[:, ncol:ncol + 1], ALU.mult, ALU.add), s_v)
                        m[("scan", gseq)] = s_v.n
                        if nl % 2 == 1:
                            pseq = gseq // 2
                            wslot = pseq % 2
                            wait(VE, "v", s_v, m[("scan", gseq)])
                            inc(VE.tensor_copy(h0[:, ncol - 1:ncol],
                                               h_t[:, TT - 1: TT]), s_v)
                            m[("h0m", ncol - 1)] = s_v.n
                            inc(VE.tensor_copy(h0[:, ncol:ncol + 1],
                                               h_t[:, 2 * TT - 1: 2 * TT]), s_v)
                            m[("h0m", ncol)] = s_v.n
                            wait(VE, "v", s_s, m[("bce", pidx, 8 + nl)])
                            wait(VE, "v", s_p, m.get(("yadd", pseq - 2), 0))
                            inc(VE.tensor_tensor(
                                w_t[:, :], h_t[:, :],
                                bct[:, (8 + nl - 1) * TT:(8 + nl + 1) * TT],
                                ALU.mult), s_v)
                            m[("wmul", pseq)] = s_v.n
                            wait(VE, "v", s_v, m[("wmul", pseq)])
                            inc(VE.tensor_tensor(ws_t[:, wslot * TT:(wslot + 1) * TT],
                                                 w_t[:, 0:TT], w_t[:, TT:2 * TT],
                                                 ALU.add), s_v)
                            m[("wsum", pseq)] = s_v.n
                            wait(PO, "p", s_v, m[("wsum", pseq)])
                            if nh == 0:
                                if nl == 1:
                                    wait(PO, "p", s_s, m[("yi", k, dc)])
                                wait(PO, "p", s_p, m.get(("ylast", k, dc), 0))
                                inc(PO.tensor_tensor(y_sb[:, dc * TT:(dc + 1) * TT],
                                                     y_sb[:, dc * TT:(dc + 1) * TT],
                                                     ws_t[:, wslot * TT:(wslot + 1) * TT],
                                                     ALU.add), s_p)
                            else:
                                y2s = y2_sb[:, (dc % 2) * TT:((dc % 2) + 1) * TT]
                                wait(PO, "p", s_p, m.get(("ylast2", k, dc), 0))
                                if nl == 1:
                                    wait(PO, "p", s_p, m.get(("merge", k, dc - 2), 0))
                                    inc(PO.tensor_copy(y2s,
                                                       ws_t[:, wslot * TT:(wslot + 1) * TT]), s_p)
                                else:
                                    inc(PO.tensor_tensor(y2s, y2s,
                                                         ws_t[:, wslot * TT:(wslot + 1) * TT],
                                                         ALU.add), s_p)
                            m[("yadd", pseq)] = s_p.n
                            if nh == 0:
                                m[("ylast", k, dc)] = s_p.n
                            else:
                                m[("ylast2", k, dc)] = s_p.n

                    if nh == 1:
                        wait(PO, "p", s_dne, k * 32 + dc + 1)
                        wait(PO, "p", s_p, m[("ylast2", k, dc)])
                        y2s = y2_sb[:, (dc % 2) * TT:((dc % 2) + 1) * TT]
                        inc(PO.tensor_tensor(y_sb[:, dc * TT:(dc + 1) * TT],
                                             y_sb[:, dc * TT:(dc + 1) * TT],
                                             y2s, ALU.add), s_p)
                        m[("merge", k, dc)] = s_p.n
                    prep_pump(4)

                m[("spall", pidx)] = s_s.n
                m[("poolp", pidx)] = s_p.n
                m[("multp", pidx)] = s_v.n
                if dbg_phase <= 3:
                    break
                if nh == 0:
                    for dcp in range(NDC):
                        _dne0(VE, PO, wait, inc, m, s_dng, s_p, s_v, s_dne,
                              y_sb, ps_dn, dsos, k, dcp, TT)

                half = nh
                for jcl in range(32):
                    jcg = half * 32 + jcl
                    gidx = k * NJC + jcg
                    wait(SC, "s", s_upg, gidx + 1)
                    if half == 1:
                        wait(SC, "s", s_dng, ((k * 2 + 0) * NDC + 15) * 2 + (jcl // 16) + 1)
                    elif k > 0:
                        wait(SC, "s", s_dng, ((k * 2 - 1) * NDC + 15) * 2 + (jcl // 16) + 1)
                    inc(SC.activation(hid[:, jcl * TT:(jcl + 1) * TT],
                                      ps_hid[:, (jcg % 2) * 512:(jcg % 2 + 1) * 512],
                                      AF.Relu, scale=usc_sb[:, jcg:jcg + 1]), s_rle)

                prep_pump_until(64 if half == 0 else 192)
                for jcl in range(32):
                    jcg = half * 32 + jcl
                    gidx = k * NJC + jcg
                    wait(SY, "sy", s_upg, gidx - 1)
                    scr_up_ready(SY, "sy", jcg)
                    inc(SY.dma_start(wup[:, (jcg % 2) * 2048:(jcg % 2 + 1) * 2048],
                                     scr_up[jcg, :, :]), s_wu[gidx % 2], 16)
                    wait(TE, "t", s_wu[gidx % 2], 16 * (gidx // 2 + 1))
                    wait(TE, "t", s_rle, gidx - 1)
                    for dk in range(NDC):
                        instr = TE.matmul(
                            ps_hid[:, (jcg % 2) * 512:(jcg % 2 + 1) * 512],
                            wup[:, (jcg % 2) * 2048 + dk * 128:
                                (jcg % 2) * 2048 + (dk + 1) * 128],
                            xn_d[:, dk * TT:(dk + 1) * TT],
                            start=(dk == 0), stop=(dk == NDC - 1))
                    inc(instr, s_upg)

                if dbg_phase <= 4:
                    break
                prep_pump_until(128 if half == 0 else 256)
                for dc in range(NDC):
                    didx = (k * 2 + half) * NDC + dc
                    for c in range(2):
                        cidx = didx * 2 + c
                        wait(SY, "sy", s_dng, cidx - 1)
                        scr_dn_ready(SY, "sy", half * 16 + dc, c)
                        inc(SY.dma_start(
                            wdn[:, (cidx % 2) * 2048:((cidx % 2) + 1) * 2048],
                            scr_dn[half * 16 + dc, :, c * 2048:(c + 1) * 2048]),
                            s_wd[cidx % 2], 16)
                        m[("wdn", cidx)] = 16 * (cidx // 2 + 1)
                    for c in range(2):
                        cidx = didx * 2 + c
                        wait(TE, "t", s_wd[cidx % 2], m[("wdn", cidx)])
                        wait(TE, "t", s_rle, k * NJC + half * 32 + 16 * (c + 1))
                        if c == 0:
                            wait(TE, "t", s_dne, k * 32 + half * 16 + dc - 1)
                            if k > 0 and half == 0 and dc < 2:
                                wait(TE, "t", s_v, m.get(("epiadd", k - 1, 3, 3), 0))
                        for j in range(16):
                            jcl = c * 16 + j
                            instr = TE.matmul(
                                ps_dn[:, (dc % 2) * 512:(dc % 2 + 1) * 512],
                                wdn[:, (cidx % 2) * 2048 + j * 128:
                                    (cidx % 2) * 2048 + (j + 1) * 128],
                                hid[:, jcl * TT:(jcl + 1) * TT],
                                start=(jcl == 0), stop=(jcl == 31))
                        inc(instr, s_dng)

            if dbg_phase <= 3:
                continue
            if dbg_phase <= 4:
                continue
            if dbg_phase <= 6:
                continue
            for dc in range(NDC):
                _dne1(VE, wait, inc, m, s_dng, s_p, s_v, s_dne,
                      y_sb, ps_dn, dsos, k, dc, TT)
            if dbg_phase <= 7:
                continue

            for tsub in range(NSUB):
                slot = tsub % 2
                if tsub < 2:
                    wait(SY, "sy", s_s, m[("xa", k * 4 + tsub + 2)])
                else:
                    wait(SY, "sy", s_v, m.get(("epiadd", k, tsub - 2, 3), 0))
                inc(SY.dma_start(x_t[:, slot * D:(slot + 1) * D],
                                 x_d[row0 + tsub * 128: row0 + (tsub + 1) * 128, :]),
                    s_x[slot], 16)
                xcnt[slot] += 1
                m[("x2ld", k, tsub)] = (slot, 16 * xcnt[slot])
                for g in range(4):
                    pslot = g % 2
                    wait(TE, "t", s_dne, 32 * (k + 1))
                    prev = (m.get(("epiadd", k, tsub, g - 2), 0) if g >= 2
                            else m.get(("epiadd", k, tsub - 1, g + 2), 0))
                    wait(TE, "t", s_v, prev)
                    for i in range(4):
                        dc = g * 4 + i
                        instr = TE.transpose(
                            ps_dn[:, pslot * 512 + i * 128: pslot * 512 + (i + 1) * 128],
                            y_sb[:, dc * TT + tsub * 128: dc * TT + (tsub + 1) * 128],
                            idf_sb[:, :])
                    inc(instr, s_t)
                    m[("epiT", k, tsub, g)] = s_t.n
                    oslot = (tsub * 2 + g // 2) % 2
                    wait(VE, "v", s_t, m[("epiT", k, tsub, g)])
                    xsl, xv = m[("x2ld", k, tsub)]
                    wait(VE, "v", s_x[xsl], xv)
                    if g % 2 == 0:
                        wait(VE, "v", s_st[oslot], 16 * stcnt[oslot])
                    inc(VE.tensor_tensor(
                        out_t[:, oslot * 1024 + (g % 2) * 512:
                              oslot * 1024 + (g % 2 + 1) * 512],
                        ps_dn[:, pslot * 512:(pslot + 1) * 512],
                        x_t[:, slot * D + g * 512: slot * D + (g + 1) * 512],
                        ALU.add), s_v)
                    m[("epiadd", k, tsub, g)] = s_v.n
                    if g % 2 == 1:
                        hh = g // 2
                        wait(SY, "sy", s_v, m[("epiadd", k, tsub, g)])
                        inc(SY.dma_start(
                            out_d[row0 + tsub * 128: row0 + (tsub + 1) * 128,
                                  hh * 1024:(hh + 1) * 1024],
                            out_t[:, oslot * 1024:(oslot + 1) * 1024]), s_st[oslot], 16)
                        stcnt[oslot] += 1
                m[("epiT", k)] = s_t.n
            prep_pump(8)

        prep_pump_until(PREP_TOTAL)
        SY.wait_ge(s_st[0].h, 16 * stcnt[0])
        SY.wait_ge(s_st[1].h, 16 * stcnt[1])
    return nc


def _dne0(VE, PO, wait, inc, m, s_dng, s_p, s_v, s_dne, y_sb, ps_dn, dsos, k, dc, TT):
    didx = (k * 2 + 0) * 16 + dc
    wait(VE, "v", s_dng, didx * 2 + 2)
    wait(VE, "v", s_v, m["dsosdone"])
    wait(VE, "v", s_p, m[("yadd", ((k * 2 + 0) * 16 + dc) * 4 + 3)])
    instr = VE.scalar_tensor_tensor(
        y_sb[:, dc * TT:(dc + 1) * TT],
        ps_dn[:, (dc % 2) * 512:(dc % 2 + 1) * 512],
        dsos[:, dc:dc + 1],
        y_sb[:, dc * TT:(dc + 1) * TT],
        mybir.AluOpType.mult, mybir.AluOpType.add)
    inc(instr, s_dne)


def _dne1(VE, wait, inc, m, s_dng, s_p, s_v, s_dne, y_sb, ps_dn, dsos, k, dc, TT):
    didx = (k * 2 + 1) * 16 + dc
    wait(VE, "v", s_dng, didx * 2 + 2)
    wait(VE, "v", s_v, m["dsosdone"])
    wait(VE, "v", s_p, m[("merge", k, dc)])
    wait(VE, "v", s_dne, k * 32 + dc + 1)
    instr = VE.scalar_tensor_tensor(
        y_sb[:, dc * TT:(dc + 1) * TT],
        ps_dn[:, (dc % 2) * 512:(dc % 2 + 1) * 512],
        dsos[:, dc:dc + 1],
        y_sb[:, dc * TT:(dc + 1) * TT],
        mybir.AluOpType.mult, mybir.AluOpType.add)
    inc(instr, s_dne)


def _pack_inputs(W_xproj, W_dt, b_dt, A_log, D_param, up_w, dn_w, us, ds, osf):
    f32 = np.float32
    wx = np.ascontiguousarray(
        np.asarray(W_xproj, f32).T.reshape(16, 128, 160)
        .transpose(1, 0, 2).reshape(128, 16 * 160))
    wdt = np.ascontiguousarray(np.asarray(W_dt, f32).T)
    alog = np.ascontiguousarray(
        np.asarray(A_log, f32).reshape(16, 128, 16)
        .transpose(1, 0, 2).reshape(128, 256))
    bdt = np.ascontiguousarray(np.asarray(b_dt, f32).reshape(16, 128).T)
    dpar = np.ascontiguousarray(np.asarray(D_param, f32).reshape(16, 128).T)
    usc = np.ascontiguousarray(np.asarray(us, f32).reshape(64, 128).T)
    dsc = np.ascontiguousarray(np.asarray(ds, f32).reshape(16, 128).T)
    osc = np.ascontiguousarray(
        np.broadcast_to(np.asarray(osf, f32).reshape(1, 1), (128, 1)))
    upw = np.ascontiguousarray(
        np.asarray(up_w, f32).reshape(64, 128, 16, 128)
        .transpose(0, 3, 2, 1).reshape(64, 128, 2048))
    dnw = np.ascontiguousarray(
        np.asarray(dn_w, f32).reshape(16, 128, 2, 32, 128)
        .transpose(2, 0, 4, 3, 1).reshape(32, 128, 4096))
    identf = np.eye(128, dtype=f32)
    identb = np.eye(128, dtype=ml_dtypes.bfloat16)
    cvals = np.ascontiguousarray(
        np.broadcast_to(np.array([[1e-5, -0.5, -1.0]], f32), (128, 3)))
    sels = []
    for nh in range(2):
        sel = np.zeros((32, 16, 128), dtype=ml_dtypes.bfloat16)
        for i in range(16):
            row = (nh * 8 + i) if i < 8 else (16 + nh * 8 + (i - 8))
            sel[row, i, :] = 1.0
        sels.append(np.ascontiguousarray(sel.reshape(32, 2048)))
    return dict(wx=wx, wdt=wdt, alog=alog, bdt=bdt, dpar=dpar, usc=usc,
                dsc=dsc, osc=osc, upw=upw, dnw=dnw, identf=identf,
                identb=identb, sel0=sels[0], sel1=sels[1], cvals=cvals)


def kernel(L0_attn_x, L0_attn_W_xproj, L0_attn_W_dt, L0_attn_b_dt,
           L0_attn_A_log, L0_attn_D_param, L0_ffn_up_weight,
           L0_ffn_down_weight, L0_ffn_up_scale, L0_ffn_down_scale,
           L0_ffn_output_scale):
    shared = _pack_inputs(L0_attn_W_xproj, L0_attn_W_dt, L0_attn_b_dt,
                          L0_attn_A_log, L0_attn_D_param, L0_ffn_up_weight,
                          L0_ffn_down_weight, L0_ffn_up_scale,
                          L0_ffn_down_scale, L0_ffn_output_scale)
    x = np.asarray(L0_attn_x, np.float32)
    nc = build_nc()
    in_maps = []
    for c in range(8):
        im = dict(shared)
        im["x"] = np.ascontiguousarray(x[2 * c:2 * c + 2].reshape(B_CORE * T, D))
        in_maps.append(im)
    res = run_bass_kernel_spmd(nc, in_maps, core_ids=list(range(8)))
    out = np.stack([np.asarray(r["out"], np.float32).reshape(B_CORE, T, D)
                    for r in res.results], axis=0)
    return np.ascontiguousarray(out.reshape(16, T, D))


# version 05
# speedup vs baseline: 1.3442x; 1.3442x over previous
import numpy as np
import ml_dtypes
from contextlib import ExitStack

import concourse.bass as bass
import concourse.mybir as mybir
from concourse.bass_utils import run_bass_kernel_spmd

F32 = mybir.dt.float32
BF16 = mybir.dt.bfloat16
AF = mybir.ActivationFunctionType
ALU = mybir.AluOpType

B_CORE = 2
T = 2048
D = 2048
N = 16
HID = 8192
TT = 512
NTT = (B_CORE * T) // TT
NSUB = TT // 128
NDC = D // 128
NJC = HID // 128

WMUL_VE_MOD = 8

_QPAT = [3, 3, 3, 3, 2, 2]
UPS_SCHED = [[] for _ in range(33)]
DNS_SCHED = [[] for _ in range(33)]
for _q in range(4):
    _o = 0
    for _i, _c in enumerate(_QPAT):
        for _j in range(_c):
            UPS_SCHED[6 * _q + _i].append(_q * 16 + _o)
            DNS_SCHED[6 * _q + 6 + _i].append((_q, _o))
            _o += 1


class Sem:
    def __init__(self, stack, nc, name):
        self.h = stack.enter_context(nc.semaphore(name))
        self.name = name
        self.n = 0


def build_nc():
    nc = bass.Bass("TRN2", target_bir_lowering=False)

    x_d = nc.dram_tensor("x", [B_CORE * T, D], F32, kind="ExternalInput").ap()
    wx_d = nc.dram_tensor("wx", [128, NDC * 160], F32, kind="ExternalInput").ap()
    wdt_d = nc.dram_tensor("wdt", [128, D], F32, kind="ExternalInput").ap()
    alog_d = nc.dram_tensor("alog", [128, NDC * N], F32, kind="ExternalInput").ap()
    bdt_d = nc.dram_tensor("bdt", [128, NDC], F32, kind="ExternalInput").ap()
    dpar_d = nc.dram_tensor("dpar", [128, NDC], F32, kind="ExternalInput").ap()
    usc_d = nc.dram_tensor("usc", [128, NJC], F32, kind="ExternalInput").ap()
    dsc_d = nc.dram_tensor("dsc", [128, NDC], F32, kind="ExternalInput").ap()
    osc_d = nc.dram_tensor("osc", [128, 1], F32, kind="ExternalInput").ap()
    upw_d = nc.dram_tensor("upw", [NJC, 128, 2048], F32, kind="ExternalInput").ap()
    dnw_d = nc.dram_tensor("dnw", [32, 128, 4096], F32, kind="ExternalInput").ap()
    idf_d = nc.dram_tensor("identf", [128, 128], F32, kind="ExternalInput").ap()
    idb_d = nc.dram_tensor("identb", [128, 128], BF16, kind="ExternalInput").ap()
    sel0_d = nc.dram_tensor("sel0", [32, 2048], BF16, kind="ExternalInput").ap()
    sel1_d = nc.dram_tensor("sel1", [32, 2048], BF16, kind="ExternalInput").ap()
    cv_d = nc.dram_tensor("cvals", [128, 3], F32, kind="ExternalInput").ap()
    out_d = nc.dram_tensor("out", [B_CORE * T, D], F32, kind="ExternalOutput").ap()
    scr_up = nc.dram_tensor("scr_up", [NJC, 128, 2048], BF16, kind="Internal").ap()
    scr_dn = nc.dram_tensor("scr_dn", [32, 128, 4096], BF16, kind="Internal").ap()

    stk = ExitStack()
    sb = lambda name, shape, dt: stk.enter_context(nc.sbuf_tensor(name, shape, dt))
    pst = lambda name, shape, dt: stk.enter_context(nc.psum_tensor(name, shape, dt))
    with stk:
        xn_d = sb("xn_d", [128, NDC * TT], BF16)
        y_sb = sb("y", [128, NDC * TT], F32)
        x_t = sb("x_t", [128, 2 * D], F32)
        xnt = sb("xnt", [128, D], BF16)
        junk = xnt
        ln8 = sb("ln8", [128, 32], F32)
        bct = sb("bct", [128, 16 * TT], BF16)
        dr_sb = sb("dr", [128, TT], BF16)
        bc_sb = sb("bc", [32, TT], BF16)
        sel_sb = [sb("sel0s", [32, 2048], BF16), sb("sel1s", [32, 2048], BF16)]
        delta_t = sb("delta", [128, 2 * TT], F32)
        u_t = sb("u", [128, 2 * TT], F32)
        dA_t = sb("dA", [128, 2 * TT], F32)
        Bu_t = sb("Bu", [128, 2 * TT], F32)
        h_t = sb("h", [128, 4 * TT], F32)
        w_t = sb("w", [128, 2 * 2 * TT], F32)
        y2_sb = sb("y2", [128, 2 * TT], F32)
        h0 = sb("h0", [128, NDC * N], F32)
        hid = sb("hid", [128, 32 * TT], BF16)
        wup = sb("wup", [128, 2 * 2048], BF16)
        wdn = sb("wdn", [128, 2 * 2048], BF16)
        out_t = sb("out_t", [128, 2 * 512], F32)
        wx_sb = sb("wx_sb", [128, NDC * 160], BF16)
        wdt_sb = sb("wdt_sb", [128, D], BF16)
        negA = sb("negA", [128, NDC * N], F32)
        bdt_sb = sb("bdt_sb", [128, NDC], F32)
        dpar_sb = sb("dpar_sb", [128, NDC], F32)
        usc_sb = sb("usc_sb", [128, NJC], F32)
        dsos = sb("dsos", [128, NDC], F32)
        osc_sb = sb("osc_sb", [128, 1], F32)
        cv_sb = sb("cv_sb", [128, 3], F32)
        idf_sb = sb("idf_sb", [128, 128], F32)
        idb_sb = sb("idb_sb", [128, 128], BF16)
        stg = sb("stg", [128, 2048], F32)
        sgn = sb("sgn", [128, 2 * 1024], BF16)

        ps_trb = pst("ps_trb", [128, 1024], BF16)
        ps_at = pst("ps_at", [128, 1024], F32)
        ps_hid = pst("ps_hid", [128, 1024], F32)
        ps_dn = pst("ps_dn", [128, 1536], F32)

        s_cst = Sem(stk, nc, "s_cst")
        s_pl = [Sem(stk, nc, "s_pl0"), Sem(stk, nc, "s_pl1")]
        s_sw = [Sem(stk, nc, "s_sw0"), Sem(stk, nc, "s_sw1")]
        s_x = [Sem(stk, nc, "s_x0"), Sem(stk, nc, "s_x1")]
        s_wu = [Sem(stk, nc, "s_wu0"), Sem(stk, nc, "s_wu1")]
        s_wd = [Sem(stk, nc, "s_wd0"), Sem(stk, nc, "s_wd1")]
        s_st = [Sem(stk, nc, "s_st0"), Sem(stk, nc, "s_st1")]
        s_t = Sem(stk, nc, "s_t")
        s_upg = Sem(stk, nc, "s_upg")
        s_dng = Sem(stk, nc, "s_dng")
        s_s = Sem(stk, nc, "s_s")
        s_rle = Sem(stk, nc, "s_rle")
        s_v = Sem(stk, nc, "s_v")
        s_p = Sem(stk, nc, "s_p")

        m = {}
        W = {}

        def wait(eng, ename, s, v):
            if not v or v <= 0:
                return
            key = (ename, s.name)
            if W.get(key, 0) >= v:
                return
            eng.wait_ge(s.h, v)
            W[key] = v

        def inc(instr, s, k=1):
            instr.then_inc(s.h, k)
            s.n += k
            return instr

        TE, SC, VE, PO, SY = nc.tensor, nc.scalar, nc.vector, nc.gpsimd, nc.sync

        inc(SY.dma_start(stg[:, 0:1280], wx_d[:, 0:1280]), s_cst, 16)
        inc(SY.dma_start(stg[:, 1280:2048], wx_d[:, 1280:2048]), s_cst, 16)
        inc(SY.dma_start(negA[:, :], alog_d[:, :]), s_cst, 16)
        inc(SY.dma_start(bdt_sb[:, :], bdt_d[:, :]), s_cst, 16)
        inc(SY.dma_start(dpar_sb[:, :], dpar_d[:, :]), s_cst, 16)
        inc(SY.dma_start(usc_sb[:, :], usc_d[:, :]), s_cst, 16)
        inc(SY.dma_start(dsos[:, :], dsc_d[:, :]), s_cst, 16)
        inc(SY.dma_start(osc_sb[:, :], osc_d[:, :]), s_cst, 16)
        inc(SY.dma_start(cv_sb[:, :], cv_d[:, :]), s_cst, 16)
        inc(SY.dma_start(idf_sb[:, :], idf_d[:, :]), s_cst, 16)
        inc(SY.dma_start(idb_sb[:, :], idb_d[:, :]), s_cst, 16)
        inc(SY.dma_start(sel_sb[0][:, :], sel0_d[:, :]), s_cst, 16)
        inc(SY.dma_start(sel_sb[1][:, :], sel1_d[:, :]), s_cst, 16)
        m["constld"] = s_cst.n

        SC.add_instruction(mybir.InstLoadActFuncSet(
            name=nc.get_next_instruction_name(), act_func_set_id=6, ins=[], outs=[]))

        wait(SC, "s", s_cst, m["constld"])
        inc(SC.copy(wx_sb[:, 0:1280], stg[:, 0:1280]), s_s)
        inc(SC.copy(wx_sb[:, 1280:2048], stg[:, 1280:2048]), s_s); m["wxcast0"] = s_s.n
        inc(SC.activation(negA[:, :], negA[:, :], AF.Exp), s_s); m["negAe"] = s_s.n
        wait(SC, "s", s_s, m["negAe"])
        inc(SC.mul(negA[:, :], negA[:, :], cv_sb[:, 2:3]), s_s); m["negAdone"] = s_s.n
        wait(VE, "v", s_cst, m["constld"])
        inc(VE.tensor_scalar(dsos[:, :], dsos[:, :], osc_sb[:, 0:1], None, ALU.mult), s_v)
        m["dsosdone"] = s_v.n
        wait(SY, "sy", s_s, m["wxcast0"])
        inc(SY.dma_start(stg[:, 0:512], wx_d[:, 2048:2560]), s_cst, 16); m["wxld2"] = s_cst.n
        wait(SC, "s", s_cst, m["wxld2"])
        inc(SC.copy(wx_sb[:, 2048:2560], stg[:, 0:512]), s_s); m["wxcast"] = s_s.n
        wait(SY, "sy", s_s, m["wxcast"])
        inc(SY.dma_start(stg[:, 0:2048], wdt_d[:, :]), s_cst, 16); m["wdtld"] = s_cst.n
        wait(SC, "s", s_cst, m["wdtld"])
        inc(SC.copy(wdt_sb[:, :], stg[:, 0:2048]), s_s); m["wdtcast"] = s_s.n

        PREP_TOTAL = 256
        prep = {"i": 0}
        pm_sign = {}

        def prep_src(p):
            g, r = p // 32, p % 32
            if g % 2 == 0:
                jcg = (g // 2) * 16 + r // 2
                c = r % 2
                return (upw_d[jcg, :, c * 1024:(c + 1) * 1024],
                        scr_up[jcg, :, c * 1024:(c + 1) * 1024])
            qidx = g // 2
            h2, ch = qidx // 2, qidx % 2
            dc = r // 2
            ci = ch * 2 + r % 2
            return (dnw_d[h2 * 16 + dc, :, ci * 1024:(ci + 1) * 1024],
                    scr_dn[h2 * 16 + dc, :, ci * 1024:(ci + 1) * 1024])

        def prep_pump(nchunks):
            for _ in range(nchunks):
                i = prep["i"]
                if i >= PREP_TOTAL:
                    return
                prep["i"] = i + 1
                src, dst = prep_src(i)
                slot = i % 2
                wait(SY, "sy", s_s, pm_sign.get(i - 2, 0))
                if i < 2:
                    wait(SY, "sy", s_s, m["wdtcast"])
                inc(SY.dma_start(stg[:, slot * 1024:(slot + 1) * 1024], src),
                    s_pl[slot], 16)
                wait(SC, "s", s_pl[slot], 16 * (i // 2 + 1))
                wait(SC, "s", s_sw[slot], 16 * (i // 2))
                inc(SC.sign(sgn[:, slot * 1024:(slot + 1) * 1024],
                            stg[:, slot * 1024:(slot + 1) * 1024]), s_s)
                pm_sign[i] = s_s.n
                wait(SY, "sy", s_s, pm_sign[i])
                inc(SY.dma_start(dst, sgn[:, slot * 1024:(slot + 1) * 1024]),
                    s_sw[slot], 16)

        def prep_pump_until(n):
            while prep["i"] < n:
                prep_pump(1)

        def prep_ready(eng, ename, p_end):
            wait(eng, ename, s_sw[0], 16 * ((p_end + 1) // 2))
            wait(eng, ename, s_sw[1], 16 * (p_end // 2))

        xq = []
        xtot = [0]
        xcnt = [0, 0]
        stcnt = [0, 0]

        def emit_xload(rows):
            i = xtot[0]
            slot = i % 2
            if i >= 2:
                kind, val = xq[i - 2]
                if kind == "s":
                    wait(SY, "sy", s_s, val)
                else:
                    wait(SY, "sy", s_v, val)
            inc(SY.dma_start(x_t[:, slot * D:(slot + 1) * D],
                             x_d[rows * 128:(rows + 1) * 128, :]),
                s_x[slot], 16)
            xtot[0] = i + 1
            xcnt[slot] += 1
            return slot, 16 * xcnt[slot]

        def emit_stats(k, tsub):
            xsl, xv = emit_xload(k * NSUB + tsub)
            xs = x_t[:, xsl * D:(xsl + 1) * D]
            c = tsub * 8
            wait(SC, "s", s_x[xsl], xv)
            inc(SC.activation(junk[:, :], xs, AF.Square, accum_out=ln8[:, c:c + 1]), s_s)
            wait(SC, "s", s_s, s_s.n)
            inc(SC.activation(junk[:, :], xs, AF.Copy, accum_out=ln8[:, c + 1:c + 2]), s_s)
            m[("lnacc", k, tsub)] = s_s.n
            xq.append(("s", s_s.n))
            wait(VE, "v", s_s, m[("lnacc", k, tsub)])
            inc(VE.tensor_scalar(ln8[:, c + 2:c + 3], ln8[:, c + 1:c + 2],
                                 -1.0 / D, None, ALU.mult), s_v)
            inc(VE.tensor_scalar(ln8[:, c + 3:c + 4], ln8[:, c:c + 1],
                                 1.0 / D, None, ALU.mult), s_v)
            m[("lnv1", k, tsub)] = s_v.n
            wait(SC, "s", s_v, m[("lnv1", k, tsub)])
            inc(SC.activation(ln8[:, c + 4:c + 5], ln8[:, c + 2:c + 3], AF.Square), s_s)
            m[("m2", k, tsub)] = s_s.n
            wait(VE, "v", s_s, m[("m2", k, tsub)])
            wait(VE, "v", s_v, m[("lnv1", k, tsub)])
            inc(VE.tensor_tensor(ln8[:, c + 3:c + 4], ln8[:, c + 3:c + 4],
                                 ln8[:, c + 4:c + 5], ALU.subtract), s_v)
            m[("lnv", k, tsub)] = s_v.n
            wait(SC, "s", s_v, m[("lnv", k, tsub)])
            inc(SC.activation(ln8[:, c + 5:c + 6], ln8[:, c + 3:c + 4], AF.Ln,
                              bias=cv_sb[:, 0:1]), s_s)
            wait(SC, "s", s_s, s_s.n)
            inc(SC.activation(ln8[:, c + 6:c + 7], ln8[:, c + 5:c + 6], AF.Exp,
                              scale=cv_sb[:, 1:2]), s_s)
            wait(SC, "s", s_s, s_s.n)
            inc(SC.activation(ln8[:, c + 7:c + 8], ln8[:, c + 2:c + 3], AF.Identity,
                              scale=ln8[:, c + 6:c + 7]), s_s)
            m[("stats", k, tsub)] = s_s.n

        def emit_upgroup(k, jcg):
            gidx = k * NJC + jcg
            prep_pump_until((jcg // 16) * 64 + (jcg % 16 + 1) * 2)
            wait(SY, "sy", s_upg, gidx - 1)
            prep_ready(SY, "sy", (jcg // 16) * 64 + (jcg % 16 + 1) * 2)
            inc(SY.dma_start(wup[:, (gidx % 2) * 2048:(gidx % 2 + 1) * 2048],
                             scr_up[jcg, :, :]), s_wu[gidx % 2], 16)
            wait(TE, "t", s_wu[gidx % 2], 16 * (gidx // 2 + 1))
            wait(TE, "t", s_rle, gidx - 1)
            wait(TE, "t", s_s, m[("xTeAll", k)])
            for dk in range(NDC):
                instr = TE.matmul(
                    ps_hid[:, (gidx % 2) * 512:(gidx % 2 + 1) * 512],
                    wup[:, (gidx % 2) * 2048 + dk * 128:(gidx % 2) * 2048 + (dk + 1) * 128],
                    xn_d[:, dk * TT:(dk + 1) * TT],
                    start=(dk == 0), stop=(dk == NDC - 1))
            inc(instr, s_upg)

        def emit_relu(k, jcg):
            gidx = k * NJC + jcg
            q = jcg // 16
            wait(SC, "s", s_upg, gidx + 1)
            if q >= 2:
                wait(SC, "s", s_dng, (k * 4 + q - 2) * 16 + 16)
            elif k > 0:
                wait(SC, "s", s_dng, ((k - 1) * 4 + q + 2) * 16 + 16)
            hslot = (q % 2) * 16 + (jcg % 16)
            inc(SC.activation(hid[:, hslot * TT:(hslot + 1) * TT],
                              ps_hid[:, (gidx % 2) * 512:(gidx % 2 + 1) * 512],
                              AF.Relu, scale=usc_sb[:, jcg:jcg + 1]), s_rle)

        def emit_downgroup(k, q, dc):
            widx = (k * 4 + q) * 16 + dc
            seq = 80 * k + q * 16 + dc
            prep_pump_until(PREP_TOTAL)
            wait(SY, "sy", s_dng, widx - 1)
            prep_ready(SY, "sy", q * 64 + 32 + (dc + 1) * 2)
            h2, ch = q // 2, q % 2
            inc(SY.dma_start(wdn[:, (widx % 2) * 2048:(widx % 2 + 1) * 2048],
                             scr_dn[h2 * 16 + dc, :, ch * 2048:(ch + 1) * 2048]),
                s_wd[widx % 2], 16)
            wait(TE, "t", s_wd[widx % 2], 16 * (widx // 2 + 1))
            wait(TE, "t", s_rle, k * NJC + q * 16 + 16)
            wait(TE, "t", s_v, m.get(("psdnfree", seq - 3), 0))
            for j in range(16):
                hslot = (q % 2) * 16 + j
                instr = TE.matmul(
                    ps_dn[:, (seq % 3) * 512:(seq % 3 + 1) * 512],
                    wdn[:, (widx % 2) * 2048 + j * 128:(widx % 2) * 2048 + (j + 1) * 128],
                    hid[:, hslot * TT:(hslot + 1) * TT],
                    start=(j == 0), stop=(j == 15))
            inc(instr, s_dng)
            m[("dnseq", widx)] = seq

        def emit_dne(k, q, dc):
            widx = (k * 4 + q) * 16 + dc
            seq = m[("dnseq", widx)]
            wait(VE, "v", s_dng, widx + 1)
            wait(VE, "v", s_v, m["dsosdone"])
            if q == 0:
                wait(VE, "v", s_s, m[("yi", k, dc)])
            wait(VE, "v", s_v, s_v.n)
            instr = VE.scalar_tensor_tensor(
                y_sb[:, dc * TT:(dc + 1) * TT],
                ps_dn[:, (seq % 3) * 512:(seq % 3 + 1) * 512],
                dsos[:, dc:dc + 1],
                y_sb[:, dc * TT:(dc + 1) * TT],
                ALU.mult, ALU.add)
            inc(instr, s_v)
            m[("psdnfree", seq)] = s_v.n

        def emit_wmul(k, s, p):
            gidx = (k * 32 + s) * 4 + p
            nh = s // 16
            hbase = ((2 * p) % 4) * TT
            cbase = (8 + 2 * p) * TT
            on_ve = (gidx % WMUL_VE_MOD) == WMUL_VE_MOD - 1
            eng = VE if on_ve else PO
            en = "v" if on_ve else "p"
            wait(eng, en, s_s, m[("bce", k, nh, 8 + 2 * p + 1)])
            wait(eng, en, s_v, m[("scan", k, s, 2 * p + 1)])
            if not on_ve:
                wait(PO, "p", s_v, m.get(("wfree", gidx - 2), 0))
            instr = eng.tensor_tensor(
                w_t[:, (gidx % 2) * 1024:(gidx % 2 + 1) * 1024],
                h_t[:, hbase:hbase + 2 * TT],
                bct[:, cbase:cbase + 2 * TT], ALU.mult)
            inc(instr, s_v if on_ve else s_p)
            m[("wm", k, s, p)] = (en, (s_v if on_ve else s_p).n)
            if p == 3:
                if on_ve:
                    m[("wmAllV", k, nh)] = s_v.n
                    m[("wmAllP", k, nh)] = m.get(("wmAllP", k, nh), 0)
                else:
                    m[("wmAllP", k, nh)] = s_p.n
                    m[("wmAllV", k, nh)] = m.get(("wmAllV", k, nh), 0)

        def emit_adds(k, s, plist):
            nh, dc = s // 16, s % 16
            for p in plist:
                gidx = (k * 32 + s) * 4 + p
                en, val = m[("wm", k, s, p)]
                if en == "p":
                    wait(VE, "v", s_p, val)
                else:
                    wait(VE, "v", s_v, val)
                w0 = w_t[:, (gidx % 2) * 1024:(gidx % 2) * 1024 + TT]
                w1 = w_t[:, (gidx % 2) * 1024 + TT:(gidx % 2 + 1) * 1024]
                if nh == 0:
                    if p == 0:
                        wait(VE, "v", s_s, m[("yi", k, dc)])
                    wait(VE, "v", s_v, s_v.n)
                    inc(VE.tensor_tensor(y_sb[:, dc * TT:(dc + 1) * TT],
                                         y_sb[:, dc * TT:(dc + 1) * TT], w0, ALU.add), s_v)
                    wait(VE, "v", s_v, s_v.n)
                    inc(VE.tensor_tensor(y_sb[:, dc * TT:(dc + 1) * TT],
                                         y_sb[:, dc * TT:(dc + 1) * TT], w1, ALU.add), s_v)
                else:
                    y2s = y2_sb[:, (dc % 2) * TT:(dc % 2 + 1) * TT]
                    if p == 0:
                        wait(VE, "v", s_v, s_v.n)
                        inc(VE.tensor_tensor(y2s, w0, w1, ALU.add), s_v)
                    else:
                        wait(VE, "v", s_v, s_v.n)
                        inc(VE.tensor_tensor(y2s, y2s, w0, ALU.add), s_v)
                        wait(VE, "v", s_v, s_v.n)
                        inc(VE.tensor_tensor(y2s, y2s, w1, ALU.add), s_v)
                m[("wfree", gidx)] = s_v.n
                if nh == 1 and p == 3:
                    wait(VE, "v", s_v, s_v.n)
                    inc(VE.tensor_tensor(y_sb[:, dc * TT:(dc + 1) * TT],
                                         y_sb[:, dc * TT:(dc + 1) * TT],
                                         y2s, ALU.add), s_v)
                    m[("merge", k, dc)] = s_v.n

        def emit_epilogue(k):
            for tsub in range(NSUB):
                xsl, xv = emit_xload(k * NSUB + tsub)
                for g in range(4):
                    j = tsub * 4 + g
                    seq = 80 * k + 64 + j
                    wait(TE, "t", s_v, m[("merge", k, 15)])
                    wait(TE, "t", s_v, m[("psdnfree", 80 * k + 63)])
                    wait(TE, "t", s_v, m.get(("psdnfree", seq - 3), 0))
                    for i in range(4):
                        dc = g * 4 + i
                        instr = TE.transpose(
                            ps_dn[:, (seq % 3) * 512 + i * 128:
                                  (seq % 3) * 512 + (i + 1) * 128],
                            y_sb[:, dc * TT + tsub * 128: dc * TT + (tsub + 1) * 128],
                            idf_sb[:, :])
                    inc(instr, s_t)
                    m[("epiT", k, j)] = s_t.n
                    oslot = g % 2
                    wait(VE, "v", s_t, m[("epiT", k, j)])
                    wait(VE, "v", s_x[xsl], xv)
                    wait(VE, "v", s_st[oslot], 16 * stcnt[oslot])
                    inc(VE.tensor_tensor(
                        out_t[:, oslot * 512:(oslot + 1) * 512],
                        ps_dn[:, (seq % 3) * 512:(seq % 3 + 1) * 512],
                        x_t[:, xsl * D + g * 512: xsl * D + (g + 1) * 512],
                        ALU.add), s_v)
                    m[("psdnfree", seq)] = s_v.n
                    wait(SY, "sy", s_v, s_v.n)
                    inc(SY.dma_start(
                        out_d[k * TT + tsub * 128: k * TT + (tsub + 1) * 128,
                              g * 512:(g + 1) * 512],
                        out_t[:, oslot * 512:(oslot + 1) * 512]), s_st[oslot], 16)
                    stcnt[oslot] += 1
                xq.append(("v", s_v.n))
            m[("epidone", k)] = s_v.n

        for k in range(NTT):
            if k == 0:
                for tsub in range(NSUB):
                    emit_stats(0, tsub)
            if k > 0:
                emit_epilogue(k - 1)

            for tsub in range(NSUB):
                xsl, xv = emit_xload(k * NSUB + tsub)
                c = tsub * 8
                wait(SC, "s", s_x[xsl], xv)
                wait(SC, "s", s_s, m[("stats", k, tsub)])
                wait(SC, "s", s_t, m.get(("xnT", k, tsub - 1), 0))
                inc(SC.activation(xnt[:, :], x_t[:, xsl * D:(xsl + 1) * D],
                                  AF.Identity, scale=ln8[:, c + 6:c + 7],
                                  bias=ln8[:, c + 7:c + 8]), s_s)
                m[("xa", k, tsub)] = s_s.n
                xq.append(("s", s_s.n))
                for g in range(4):
                    gseq = tsub * 4 + g
                    wait(TE, "t", s_s, m[("xa", k, tsub)])
                    wait(TE, "t", s_s, m.get(("xTe", k, gseq - 1),
                                             m.get(("xTeL", k - 1), 0)))
                    wait(TE, "t", s_cst, m["constld"])
                    if k > 0 and gseq == 0:
                        wait(TE, "t", s_upg, k * NJC)
                        wait(TE, "t", s_v, m[("uall", k - 1)])
                    for i in range(4):
                        dk = g * 4 + i
                        instr = TE.transpose(
                            ps_trb[:, i * 128:(i + 1) * 128],
                            xnt[:, dk * 128:(dk + 1) * 128], idb_sb[:, :])
                    inc(instr, s_t)
                    m[("xT", k, gseq)] = s_t.n
                    wait(SC, "s", s_t, m[("xT", k, gseq)])
                    src_ap = ps_trb[:, 0:512].rearrange(
                        "p (a b) -> p a b", a=4)
                    dst_ap = xn_d[:, :].rearrange("p (dk t) -> p dk t", dk=NDC)[
                        :, g * 4:(g + 1) * 4, tsub * 128:(tsub + 1) * 128]
                    inc(SC.copy(dst_ap, src_ap), s_s)
                    m[("xTe", k, gseq)] = s_s.n
                m[("xnT", k, tsub)] = s_t.n
            m[("xTeAll", k)] = s_s.n
            m[("xTeL", k)] = m[("xTe", k, 15)]

            wait(TE, "t", s_s, m[("xTeAll", k)])
            wait(TE, "t", s_s, m["wxcast"])
            wait(TE, "t", s_s, m.get(("spe", k - 1, 31), 0))
            for dk in range(NDC):
                instr = TE.matmul(ps_at[:, 0:512],
                                  wx_sb[:, dk * 160: dk * 160 + 128],
                                  xn_d[:, dk * TT:(dk + 1) * TT],
                                  start=(dk == 0), stop=(dk == NDC - 1))
            inc(instr, s_t); m[("xz", k)] = s_t.n
            for dk in range(NDC):
                instr = TE.matmul(ps_at[0:32, 512:1024],
                                  wx_sb[:, dk * 160 + 128: dk * 160 + 160],
                                  xn_d[:, dk * TT:(dk + 1) * TT],
                                  start=(dk == 0), stop=(dk == NDC - 1))
            inc(instr, s_t); m[("bcmm", k)] = s_t.n
            wait(SC, "s", s_t, m[("xz", k)])
            inc(SC.copy(dr_sb[:, :], ps_at[:, 0:512]), s_s)
            wait(SC, "s", s_t, m[("bcmm", k)])
            inc(SC.copy(bc_sb[:, :], ps_at[0:32, 512:1024]), s_s)
            m[("xze", k)] = s_s.n

            def emit_bcm(nh, i):
                pslot = i % 2
                wait(TE, "t", s_s, m[("xze", k)])
                wait(TE, "t", s_s, m.get(("bce", k, nh, i - 2), 0))
                if (k, nh) != (0, 0):
                    pk, pnh = (k, 0) if nh == 1 else (k - 1, 1)
                    if i < 8:
                        wait(TE, "t", s_p, m[("BuAll", pk, pnh)])
                    else:
                        wait(TE, "t", s_p, m[("wmAllP", pk, pnh)])
                        wait(TE, "t", s_v, m[("wmAllV", pk, pnh)])
                if nh == 1 and i < 2:
                    wait(TE, "t", s_s, m[("spe", k, 14 + i)])
                inc(TE.matmul(ps_at[:, pslot * 512:(pslot + 1) * 512],
                              sel_sb[nh][:, i * 128:(i + 1) * 128],
                              bc_sb[:, :], start=True, stop=True), s_t)
                m[("bcmT", k, nh, i)] = s_t.n
                wait(SC, "s", s_t, m[("bcmT", k, nh, i)])
                inc(SC.copy(bct[:, i * 512:(i + 1) * 512],
                            ps_at[:, pslot * 512:(pslot + 1) * 512]), s_s)
                m[("bce", k, nh, i)] = s_s.n

            for i in range(16):
                emit_bcm(0, i)

            if k % 4 == 0:
                wait(VE, "v", s_v, m.get(("h0done", k - 1), 0))
                inc(VE.memset(h0[:, :], 0.0), s_v)
                m[("h0ms", k)] = s_v.n

            for s in range(32):
                nh, dc = s // 16, s % 16
                pslot = dc % 2

                if s >= 1:
                    emit_adds(k, s - 1, [2])
                if nh == 1 and dc == 0:
                    emit_wmul(k, 15, 3)
                    emit_adds(k, 15, [3])
                    for i in range(16):
                        emit_bcm(1, i)

                wait(TE, "t", s_s, m[("xze", k)])
                wait(TE, "t", s_s, m[("bce", k, nh, 15)])
                wait(TE, "t", s_s, m.get(("spe", k, s - 2), 0))
                wait(TE, "t", s_s, m["wdtcast"])
                inc(TE.matmul(ps_at[:, pslot * 512:(pslot + 1) * 512],
                              wdt_sb[:, dc * 128:(dc + 1) * 128],
                              dr_sb[:, :], start=True, stop=True), s_t)
                m[("dmm", k, s)] = s_t.n

                if s >= 1:
                    for (q, ddc) in DNS_SCHED[s - 1]:
                        emit_dne(k, q, ddc)

                for jcg in UPS_SCHED[s]:
                    emit_upgroup(k, jcg)
                for (q, ddc) in DNS_SCHED[s]:
                    emit_downgroup(k, q, ddc)

                wait(SC, "s", s_t, m[("dmm", k, s)])
                if s >= 2:
                    wait(SC, "s", s_v, m[("uv", k, s - 2)])
                elif k > 0:
                    wait(SC, "s", s_v, m[("uv", k - 1, s + 30)])
                dslot = delta_t[:, pslot * TT:(pslot + 1) * TT]
                inc(SC.activation(dslot, ps_at[:, pslot * 512:(pslot + 1) * 512],
                                  AF.Exp, bias=bdt_sb[:, dc:dc + 1]), s_s)
                m[("spe", k, s)] = s_s.n
                wait(SC, "s", s_s, m[("spe", k, s)])
                inc(SC.activation(dslot, dslot, AF.Ln, bias=1.0), s_s)
                m[("sp", k, s)] = s_s.n

                if s < 8:
                    for yid in (2 * s, 2 * s + 1):
                        if k > 0:
                            wait(SC, "s", s_v, m[("epidone", k - 1)])
                        wait(SC, "s", s_s, m[("xTeAll", k)])
                        inc(SC.activation(y_sb[:, yid * TT:(yid + 1) * TT],
                                          xn_d[:, yid * TT:(yid + 1) * TT],
                                          AF.Identity,
                                          scale=dpar_sb[:, yid:yid + 1]), s_s)
                        m[("yi", k, yid)] = s_s.n

                wait(VE, "v", s_s, m[("sp", k, s)])
                if s >= 2:
                    wait(VE, "v", s_p, m[("BuStep", k, s - 2)])
                elif k > 0:
                    wait(VE, "v", s_p, m[("BuStep", k - 1, s + 30)])
                inc(VE.tensor_tensor(u_t[:, pslot * TT:(pslot + 1) * TT],
                                     dslot, xn_d[:, dc * TT:(dc + 1) * TT],
                                     ALU.mult), s_v)
                m[("uv", k, s)] = s_v.n
                if s == 31:
                    m[("uall", k)] = s_v.n

                for nl in range(8):
                    ncol = dc * N + nh * 8 + nl
                    aslot = nl % 2
                    hslot = nl % 4

                    if nl >= 2:
                        wait(SC, "s", s_v, m[("scan", k, s, nl - 2)])
                    elif s >= 1:
                        wait(SC, "s", s_v, m[("scan", k, s - 1, nl + 6)])
                    elif k > 0:
                        wait(SC, "s", s_v, m[("scan", k - 1, 31, nl + 6)])
                    wait(SC, "s", s_s, m[("sp", k, s)])
                    wait(SC, "s", s_s, m["negAdone"])
                    inc(SC.activation(dA_t[:, aslot * TT:(aslot + 1) * TT],
                                      dslot, AF.Exp,
                                      scale=negA[:, ncol:ncol + 1]), s_s)
                    m[("dA", k, s, nl)] = s_s.n

                    wait(PO, "p", s_v, m[("uv", k, s)])
                    wait(PO, "p", s_s, m[("bce", k, nh, nl)])
                    if nl >= 2:
                        wait(PO, "p", s_v, m[("scan", k, s, nl - 2)])
                    elif s >= 1:
                        wait(PO, "p", s_v, m[("scan", k, s - 1, nl + 6)])
                    inc(PO.tensor_tensor(Bu_t[:, aslot * TT:(aslot + 1) * TT],
                                         u_t[:, pslot * TT:(pslot + 1) * TT],
                                         bct[:, nl * TT:(nl + 1) * TT],
                                         ALU.mult), s_p)
                    m[("Bu", k, s, nl)] = s_p.n
                    if nl == 7:
                        m[("BuStep", k, s)] = s_p.n
                        m[("BuAll", k, nh)] = s_p.n

                    if nl == 1 and s >= 1 and s != 16:
                        emit_wmul(k, s - 1, 3)
                    if nl == 2 and s >= 1 and s != 16:
                        emit_adds(k, s - 1, [3])

                    wait(VE, "v", s_s, m[("dA", k, s, nl)])
                    wait(VE, "v", s_p, m[("Bu", k, s, nl)])
                    if k % 4 == 0:
                        wait(VE, "v", s_v, m[("h0ms", k)])
                    if nl < 4:
                        if s >= 1:
                            en, val = m[("wm", k, s - 1, (nl + 4) // 2)]
                            if en == "p":
                                wait(VE, "v", s_p, val)
                        elif k > 0:
                            en, val = m[("wm", k - 1, 31, (nl + 4) // 2)]
                            if en == "p":
                                wait(VE, "v", s_p, val)
                    else:
                        en, val = m[("wm", k, s, (nl - 4) // 2)]
                        if en == "p":
                            wait(VE, "v", s_p, val)
                    wait(VE, "v", s_v, m.get(("h0c", ncol), 0))
                    inc(VE.tensor_tensor_scan(
                        h_t[:, hslot * TT:(hslot + 1) * TT],
                        dA_t[:, aslot * TT:(aslot + 1) * TT],
                        Bu_t[:, aslot * TT:(aslot + 1) * TT],
                        h0[:, ncol:ncol + 1], ALU.mult, ALU.add), s_v)
                    m[("scan", k, s, nl)] = s_v.n
                    if nl % 2 == 1:
                        wait(VE, "v", s_v, m[("scan", k, s, nl)])
                        inc(VE.tensor_copy(h0[:, ncol - 1:ncol],
                                           h_t[:, (hslot - 1) * TT + TT - 1:
                                               (hslot - 1) * TT + TT]), s_v)
                        inc(VE.tensor_copy(h0[:, ncol:ncol + 1],
                                           h_t[:, hslot * TT + TT - 1:
                                               hslot * TT + TT]), s_v)
                        m[("h0c", ncol - 1)] = s_v.n - 1
                        m[("h0c", ncol)] = s_v.n

                    if nl in (3, 5, 7) and (nl - 3) // 2 < len(UPS_SCHED[s]):
                        emit_relu(k, UPS_SCHED[s][(nl - 3) // 2])
                    if nl in (3, 5, 7):
                        emit_wmul(k, s, (nl - 3) // 2)
                    if nl == 4:
                        emit_adds(k, s, [0])
                    if nl == 6:
                        emit_adds(k, s, [1])

                m[("h0done", k)] = s_v.n

                if 14 <= s <= 17 and k + 1 < NTT:
                    emit_stats(k + 1, s - 14)

                prep_pump(6)

            emit_wmul(k, 31, 3)
            emit_adds(k, 31, [2, 3])
            for (q, ddc) in DNS_SCHED[31]:
                emit_dne(k, q, ddc)

        emit_epilogue(NTT - 1)

        prep_pump_until(PREP_TOTAL)
        SY.wait_ge(s_st[0].h, 16 * stcnt[0])
        SY.wait_ge(s_st[1].h, 16 * stcnt[1])
    return nc


def _pack_inputs(W_xproj, W_dt, b_dt, A_log, D_param, up_w, dn_w, us, ds, osf):
    f32 = np.float32
    wx = np.ascontiguousarray(
        np.asarray(W_xproj, f32).T.reshape(16, 128, 160)
        .transpose(1, 0, 2).reshape(128, 16 * 160))
    wdt = np.ascontiguousarray(np.asarray(W_dt, f32).T)
    alog = np.ascontiguousarray(
        np.asarray(A_log, f32).reshape(16, 128, 16)
        .transpose(1, 0, 2).reshape(128, 256))
    bdt = np.ascontiguousarray(np.asarray(b_dt, f32).reshape(16, 128).T)
    dpar = np.ascontiguousarray(np.asarray(D_param, f32).reshape(16, 128).T)
    usc = np.ascontiguousarray(np.asarray(us, f32).reshape(64, 128).T)
    dsc = np.ascontiguousarray(np.asarray(ds, f32).reshape(16, 128).T)
    osc = np.ascontiguousarray(
        np.broadcast_to(np.asarray(osf, f32).reshape(1, 1), (128, 1)))
    upw = np.ascontiguousarray(
        np.asarray(up_w, f32).reshape(64, 128, 16, 128)
        .transpose(0, 3, 2, 1).reshape(64, 128, 2048))
    dnw = np.ascontiguousarray(
        np.asarray(dn_w, f32).reshape(16, 128, 2, 32, 128)
        .transpose(2, 0, 4, 3, 1).reshape(32, 128, 4096))
    identf = np.eye(128, dtype=f32)
    identb = np.eye(128, dtype=ml_dtypes.bfloat16)
    cvals = np.ascontiguousarray(
        np.broadcast_to(np.array([[1e-5, -0.5, -1.0]], f32), (128, 3)))
    sels = []
    for nh in range(2):
        sel = np.zeros((32, 16, 128), dtype=ml_dtypes.bfloat16)
        for i in range(16):
            row = (nh * 8 + i) if i < 8 else (16 + nh * 8 + (i - 8))
            sel[row, i, :] = 1.0
        sels.append(np.ascontiguousarray(sel.reshape(32, 2048)))
    return dict(wx=wx, wdt=wdt, alog=alog, bdt=bdt, dpar=dpar, usc=usc,
                dsc=dsc, osc=osc, upw=upw, dnw=dnw, identf=identf,
                identb=identb, sel0=sels[0], sel1=sels[1], cvals=cvals)


def kernel(L0_attn_x, L0_attn_W_xproj, L0_attn_W_dt, L0_attn_b_dt,
           L0_attn_A_log, L0_attn_D_param, L0_ffn_up_weight,
           L0_ffn_down_weight, L0_ffn_up_scale, L0_ffn_down_scale,
           L0_ffn_output_scale):
    shared = _pack_inputs(L0_attn_W_xproj, L0_attn_W_dt, L0_attn_b_dt,
                          L0_attn_A_log, L0_attn_D_param, L0_ffn_up_weight,
                          L0_ffn_down_weight, L0_ffn_up_scale,
                          L0_ffn_down_scale, L0_ffn_output_scale)
    x = np.asarray(L0_attn_x, np.float32)
    nc = build_nc()
    in_maps = []
    for c in range(8):
        im = dict(shared)
        im["x"] = np.ascontiguousarray(x[2 * c:2 * c + 2].reshape(B_CORE * T, D))
        in_maps.append(im)
    res = run_bass_kernel_spmd(nc, in_maps, core_ids=list(range(8)))
    out = np.stack([np.asarray(r["out"], np.float32).reshape(B_CORE, T, D)
                    for r in res.results], axis=0)
    return np.ascontiguousarray(out.reshape(16, T, D))


# version 06 (final)
# speedup vs baseline: 1.4656x; 1.0903x over previous
import numpy as np
import ml_dtypes
from contextlib import ExitStack

import concourse.bass as bass
import concourse.mybir as mybir
from concourse.bass_utils import run_bass_kernel_spmd

F32 = mybir.dt.float32
BF16 = mybir.dt.bfloat16
AF = mybir.ActivationFunctionType
ALU = mybir.AluOpType

B_CORE = 2
T = 2048
D = 2048
N = 16
HID = 8192
TT = 512
NTT = (B_CORE * T) // TT
NSUB = TT // 128
NDC = D // 128
NJC = HID // 128

WMUL_VE_MOD = 8

_QPAT = [3, 3, 3, 3, 2, 2]
UPS_SCHED = [[] for _ in range(33)]
DNS_SCHED = [[] for _ in range(33)]
for _q in range(4):
    _o = 0
    for _i, _c in enumerate(_QPAT):
        for _j in range(_c):
            UPS_SCHED[6 * _q + _i].append(_q * 16 + _o)
            DNS_SCHED[6 * _q + 6 + _i].append((_q, _o))
            _o += 1


class Sem:
    def __init__(self, stack, nc, name):
        self.h = stack.enter_context(nc.semaphore(name))
        self.name = name
        self.n = 0


def build_nc():
    nc = bass.Bass("TRN2", target_bir_lowering=False)

    x_d = nc.dram_tensor("x", [B_CORE * T, D], F32, kind="ExternalInput").ap()
    wx_d = nc.dram_tensor("wx", [128, NDC * 160], F32, kind="ExternalInput").ap()
    wdt_d = nc.dram_tensor("wdt", [128, D], F32, kind="ExternalInput").ap()
    alog_d = nc.dram_tensor("alog", [128, NDC * N], F32, kind="ExternalInput").ap()
    bdt_d = nc.dram_tensor("bdt", [128, NDC], F32, kind="ExternalInput").ap()
    dpar_d = nc.dram_tensor("dpar", [128, NDC], F32, kind="ExternalInput").ap()
    usc_d = nc.dram_tensor("usc", [128, NJC], F32, kind="ExternalInput").ap()
    dsc_d = nc.dram_tensor("dsc", [128, NDC], F32, kind="ExternalInput").ap()
    osc_d = nc.dram_tensor("osc", [128, 1], F32, kind="ExternalInput").ap()
    upw_d = nc.dram_tensor("upw", [NJC, 128, 2048], F32, kind="ExternalInput").ap()
    dnw_d = nc.dram_tensor("dnw", [32, 128, 4096], F32, kind="ExternalInput").ap()
    idf_d = nc.dram_tensor("identf", [128, 128], F32, kind="ExternalInput").ap()
    idb_d = nc.dram_tensor("identb", [128, 128], BF16, kind="ExternalInput").ap()
    sel0_d = nc.dram_tensor("sel0", [32, 2048], BF16, kind="ExternalInput").ap()
    sel1_d = nc.dram_tensor("sel1", [32, 2048], BF16, kind="ExternalInput").ap()
    cv_d = nc.dram_tensor("cvals", [128, 3], F32, kind="ExternalInput").ap()
    out_d = nc.dram_tensor("out", [B_CORE * T, D], F32, kind="ExternalOutput").ap()
    scr_up = nc.dram_tensor("scr_up", [NJC, 128, 2048], BF16, kind="Internal").ap()
    scr_dn = nc.dram_tensor("scr_dn", [32, 128, 4096], BF16, kind="Internal").ap()

    stk = ExitStack()
    sb = lambda name, shape, dt: stk.enter_context(nc.sbuf_tensor(name, shape, dt))
    pst = lambda name, shape, dt: stk.enter_context(nc.psum_tensor(name, shape, dt))
    with stk:
        xn_d = sb("xn_d", [128, NDC * TT], BF16)
        y_sb = sb("y", [128, NDC * TT], F32)
        x_t = sb("x_t", [128, 2 * D], F32)
        xnt = sb("xnt", [128, D], BF16)
        junk = xnt
        ln8 = sb("ln8", [128, 32], F32)
        bct = sb("bct", [128, 16 * TT], BF16)
        dr_sb = sb("dr", [128, TT], BF16)
        bc_sb = sb("bc", [32, TT], BF16)
        sel_sb = [sb("sel0s", [32, 2048], BF16), sb("sel1s", [32, 2048], BF16)]
        delta_t = sb("delta", [128, 2 * TT], F32)
        u_t = sb("u", [128, 2 * TT], F32)
        dA_t = sb("dA", [128, 2 * TT], F32)
        Bu_t = sb("Bu", [128, 2 * TT], F32)
        h_t = sb("h", [128, 4 * TT], F32)
        w_t = sb("w", [128, 2 * 2 * TT], F32)
        y2_sb = sb("y2", [128, 2 * TT], F32)
        h0 = sb("h0", [128, NDC * N], F32)
        hid = sb("hid", [128, 32 * TT], BF16)
        wup = sb("wup", [128, 2 * 2048], BF16)
        wdn = sb("wdn", [128, 2 * 2048], BF16)
        out_t = sb("out_t", [128, 2 * 512], F32)
        wx_sb = sb("wx_sb", [128, NDC * 160], BF16)
        wdt_sb = sb("wdt_sb", [128, D], BF16)
        negA = sb("negA", [128, NDC * N], F32)
        bdt_sb = sb("bdt_sb", [128, NDC], F32)
        dpar_sb = sb("dpar_sb", [128, NDC], F32)
        usc_sb = sb("usc_sb", [128, NJC], F32)
        dsos = sb("dsos", [128, NDC], F32)
        osc_sb = sb("osc_sb", [128, 1], F32)
        cv_sb = sb("cv_sb", [128, 3], F32)
        idf_sb = sb("idf_sb", [128, 128], F32)
        idb_sb = sb("idb_sb", [128, 128], BF16)
        stg = sb("stg", [128, 2048], F32)
        sgn = sb("sgn", [128, 2 * 1024], BF16)

        ps_trb = pst("ps_trb", [128, 1024], BF16)
        ps_at = pst("ps_at", [128, 1024], F32)
        ps_hid = pst("ps_hid", [128, 1024], F32)
        ps_dn = pst("ps_dn", [128, 1536], F32)

        s_cst = Sem(stk, nc, "s_cst")
        s_pl = [Sem(stk, nc, "s_pl0"), Sem(stk, nc, "s_pl1")]
        s_sw = [Sem(stk, nc, "s_sw0"), Sem(stk, nc, "s_sw1")]
        s_x = [Sem(stk, nc, "s_x0"), Sem(stk, nc, "s_x1")]
        s_wu = [Sem(stk, nc, "s_wu0"), Sem(stk, nc, "s_wu1")]
        s_wd = [Sem(stk, nc, "s_wd0"), Sem(stk, nc, "s_wd1")]
        s_st = [Sem(stk, nc, "s_st0"), Sem(stk, nc, "s_st1")]
        s_t = Sem(stk, nc, "s_t")
        s_upg = Sem(stk, nc, "s_upg")
        s_dng = Sem(stk, nc, "s_dng")
        s_s = Sem(stk, nc, "s_s")
        s_rle = Sem(stk, nc, "s_rle")
        s_v = Sem(stk, nc, "s_v")
        s_p = Sem(stk, nc, "s_p")

        m = {}
        W = {}

        def wait(eng, ename, s, v):
            if not v or v <= 0:
                return
            key = (ename, s.name)
            if W.get(key, 0) >= v:
                return
            eng.wait_ge(s.h, v)
            W[key] = v

        def inc(instr, s, k=1):
            instr.then_inc(s.h, k)
            s.n += k
            return instr

        TE, SC, VE, PO, SY = nc.tensor, nc.scalar, nc.vector, nc.gpsimd, nc.sync

        inc(SY.dma_start(stg[:, 0:1280], wx_d[:, 0:1280]), s_cst, 16)
        inc(SY.dma_start(stg[:, 1280:2048], wx_d[:, 1280:2048]), s_cst, 16)
        inc(SY.dma_start(negA[:, :], alog_d[:, :]), s_cst, 16)
        inc(SY.dma_start(bdt_sb[:, :], bdt_d[:, :]), s_cst, 16)
        inc(SY.dma_start(dpar_sb[:, :], dpar_d[:, :]), s_cst, 16)
        inc(SY.dma_start(usc_sb[:, :], usc_d[:, :]), s_cst, 16)
        inc(SY.dma_start(dsos[:, :], dsc_d[:, :]), s_cst, 16)
        inc(SY.dma_start(osc_sb[:, :], osc_d[:, :]), s_cst, 16)
        inc(SY.dma_start(cv_sb[:, :], cv_d[:, :]), s_cst, 16)
        inc(SY.dma_start(idf_sb[:, :], idf_d[:, :]), s_cst, 16)
        inc(SY.dma_start(idb_sb[:, :], idb_d[:, :]), s_cst, 16)
        inc(SY.dma_start(sel_sb[0][:, :], sel0_d[:, :]), s_cst, 16)
        inc(SY.dma_start(sel_sb[1][:, :], sel1_d[:, :]), s_cst, 16)
        m["constld"] = s_cst.n

        SC.add_instruction(mybir.InstLoadActFuncSet(
            name=nc.get_next_instruction_name(), act_func_set_id=6, ins=[], outs=[]))

        wait(SC, "s", s_cst, m["constld"])
        inc(SC.copy(wx_sb[:, 0:1280], stg[:, 0:1280]), s_s)
        inc(SC.copy(wx_sb[:, 1280:2048], stg[:, 1280:2048]), s_s); m["wxcast0"] = s_s.n
        inc(SC.activation(negA[:, :], negA[:, :], AF.Exp), s_s); m["negAe"] = s_s.n
        wait(SC, "s", s_s, m["negAe"])
        inc(SC.mul(negA[:, :], negA[:, :], cv_sb[:, 2:3]), s_s); m["negAdone"] = s_s.n
        wait(VE, "v", s_cst, m["constld"])
        inc(VE.tensor_scalar(dsos[:, :], dsos[:, :], osc_sb[:, 0:1], None, ALU.mult), s_v)
        m["dsosdone"] = s_v.n
        wait(SY, "sy", s_s, m["wxcast0"])
        inc(SY.dma_start(stg[:, 0:512], wx_d[:, 2048:2560]), s_cst, 16); m["wxld2"] = s_cst.n
        wait(SC, "s", s_cst, m["wxld2"])
        inc(SC.copy(wx_sb[:, 2048:2560], stg[:, 0:512]), s_s); m["wxcast"] = s_s.n
        wait(SY, "sy", s_s, m["wxcast"])
        inc(SY.dma_start(stg[:, 0:2048], wdt_d[:, :]), s_cst, 16); m["wdtld"] = s_cst.n
        wait(SC, "s", s_cst, m["wdtld"])
        inc(SC.copy(wdt_sb[:, :], stg[:, 0:2048]), s_s); m["wdtcast"] = s_s.n

        PREP_TOTAL = 256
        prep = {"sg": 0, "ld": 0}
        pm_sign = {}

        def prep_src(p):
            g, r = p // 32, p % 32
            if g % 2 == 0:
                jcg = (g // 2) * 16 + r // 2
                c = r % 2
                return (upw_d[jcg, :, c * 1024:(c + 1) * 1024],
                        scr_up[jcg, :, c * 1024:(c + 1) * 1024])
            qidx = g // 2
            h2, ch = qidx // 2, qidx % 2
            dc = r // 2
            ci = ch * 2 + r % 2
            return (dnw_d[h2 * 16 + dc, :, ci * 1024:(ci + 1) * 1024],
                    scr_dn[h2 * 16 + dc, :, ci * 1024:(ci + 1) * 1024])

        def _prep_load(i):
            src, _ = prep_src(i)
            slot = i % 2
            inc(SC.dma_start(stg[:, slot * 1024:(slot + 1) * 1024], src),
                s_pl[slot], 16)

        def prep_pump(nchunks):
            for _ in range(nchunks):
                i = prep["sg"]
                if i >= PREP_TOTAL:
                    return
                while prep["ld"] < min(i + 2, PREP_TOTAL):
                    _prep_load(prep["ld"])
                    prep["ld"] += 1
                prep["sg"] = i + 1
                _, dst = prep_src(i)
                slot = i % 2
                wait(SC, "s", s_pl[slot], 16 * (i // 2 + 1))
                wait(SC, "s", s_sw[slot], 16 * (i // 2))
                inc(SC.sign(sgn[:, slot * 1024:(slot + 1) * 1024],
                            stg[:, slot * 1024:(slot + 1) * 1024]), s_s)
                pm_sign[i] = s_s.n
                wait(SC, "s", s_s, pm_sign[i])
                inc(SC.dma_start(dst, sgn[:, slot * 1024:(slot + 1) * 1024]),
                    s_sw[slot], 16)

        def prep_pump_until(n):
            while prep["sg"] < n:
                prep_pump(1)

        def prep_ready(eng, ename, p_end):
            wait(eng, ename, s_sw[0], 16 * ((p_end + 1) // 2))
            wait(eng, ename, s_sw[1], 16 * (p_end // 2))

        xq = []
        xtot = [0]
        xcnt = [0, 0]
        stcnt = [0, 0]

        def emit_xload(rows):
            i = xtot[0]
            slot = i % 2
            if i >= 2:
                kind, val = xq[i - 2]
                if kind == "s":
                    wait(SY, "sy", s_s, val)
                else:
                    wait(SY, "sy", s_v, val)
            inc(SY.dma_start(x_t[:, slot * D:(slot + 1) * D],
                             x_d[rows * 128:(rows + 1) * 128, :]),
                s_x[slot], 16)
            xtot[0] = i + 1
            xcnt[slot] += 1
            return slot, 16 * xcnt[slot]

        def emit_stats(k, tsub):
            xsl, xv = emit_xload(k * NSUB + tsub)
            xs = x_t[:, xsl * D:(xsl + 1) * D]
            c = tsub * 8
            wait(SC, "s", s_x[xsl], xv)
            inc(SC.activation(junk[:, :], xs, AF.Square, accum_out=ln8[:, c:c + 1]), s_s)
            wait(SC, "s", s_s, s_s.n)
            inc(SC.activation(junk[:, :], xs, AF.Copy, accum_out=ln8[:, c + 1:c + 2]), s_s)
            m[("lnacc", k, tsub)] = s_s.n
            xq.append(("s", s_s.n))
            wait(VE, "v", s_s, m[("lnacc", k, tsub)])
            inc(VE.tensor_scalar(ln8[:, c + 2:c + 3], ln8[:, c + 1:c + 2],
                                 -1.0 / D, None, ALU.mult), s_v)
            inc(VE.tensor_scalar(ln8[:, c + 3:c + 4], ln8[:, c:c + 1],
                                 1.0 / D, None, ALU.mult), s_v)
            m[("lnv1", k, tsub)] = s_v.n
            wait(SC, "s", s_v, m[("lnv1", k, tsub)])
            inc(SC.activation(ln8[:, c + 4:c + 5], ln8[:, c + 2:c + 3], AF.Square), s_s)
            m[("m2", k, tsub)] = s_s.n
            wait(VE, "v", s_s, m[("m2", k, tsub)])
            wait(VE, "v", s_v, m[("lnv1", k, tsub)])
            inc(VE.tensor_tensor(ln8[:, c + 3:c + 4], ln8[:, c + 3:c + 4],
                                 ln8[:, c + 4:c + 5], ALU.subtract), s_v)
            m[("lnv", k, tsub)] = s_v.n
            wait(SC, "s", s_v, m[("lnv", k, tsub)])
            inc(SC.activation(ln8[:, c + 5:c + 6], ln8[:, c + 3:c + 4], AF.Ln,
                              bias=cv_sb[:, 0:1]), s_s)
            wait(SC, "s", s_s, s_s.n)
            inc(SC.activation(ln8[:, c + 6:c + 7], ln8[:, c + 5:c + 6], AF.Exp,
                              scale=cv_sb[:, 1:2]), s_s)
            wait(SC, "s", s_s, s_s.n)
            inc(SC.activation(ln8[:, c + 7:c + 8], ln8[:, c + 2:c + 3], AF.Identity,
                              scale=ln8[:, c + 6:c + 7]), s_s)
            m[("stats", k, tsub)] = s_s.n

        def emit_upgroup(k, jcg):
            gidx = k * NJC + jcg
            prep_pump_until((jcg // 16) * 64 + (jcg % 16 + 1) * 2)
            wait(SY, "sy", s_upg, gidx - 1)
            prep_ready(SY, "sy", (jcg // 16) * 64 + (jcg % 16 + 1) * 2)
            inc(SY.dma_start(wup[:, (gidx % 2) * 2048:(gidx % 2 + 1) * 2048],
                             scr_up[jcg, :, :]), s_wu[gidx % 2], 16)
            wait(TE, "t", s_wu[gidx % 2], 16 * (gidx // 2 + 1))
            wait(TE, "t", s_rle, gidx - 1)
            wait(TE, "t", s_s, m[("xTeAll", k)])
            for dk in range(NDC):
                instr = TE.matmul(
                    ps_hid[:, (gidx % 2) * 512:(gidx % 2 + 1) * 512],
                    wup[:, (gidx % 2) * 2048 + dk * 128:(gidx % 2) * 2048 + (dk + 1) * 128],
                    xn_d[:, dk * TT:(dk + 1) * TT],
                    start=(dk == 0), stop=(dk == NDC - 1))
            inc(instr, s_upg)

        def emit_relu(k, jcg):
            gidx = k * NJC + jcg
            q = jcg // 16
            wait(SC, "s", s_upg, gidx + 1)
            if q >= 2:
                wait(SC, "s", s_dng, (k * 4 + q - 2) * 16 + 16)
            elif k > 0:
                wait(SC, "s", s_dng, ((k - 1) * 4 + q + 2) * 16 + 16)
            hslot = (q % 2) * 16 + (jcg % 16)
            inc(SC.activation(hid[:, hslot * TT:(hslot + 1) * TT],
                              ps_hid[:, (gidx % 2) * 512:(gidx % 2 + 1) * 512],
                              AF.Relu, scale=usc_sb[:, jcg:jcg + 1]), s_rle)

        def emit_downgroup(k, q, dc):
            widx = (k * 4 + q) * 16 + dc
            seq = 80 * k + q * 16 + dc
            prep_pump_until(q * 64 + 32 + (dc + 1) * 2)
            wait(SY, "sy", s_dng, widx - 1)
            prep_ready(SY, "sy", q * 64 + 32 + (dc + 1) * 2)
            h2, ch = q // 2, q % 2
            inc(SY.dma_start(wdn[:, (widx % 2) * 2048:(widx % 2 + 1) * 2048],
                             scr_dn[h2 * 16 + dc, :, ch * 2048:(ch + 1) * 2048]),
                s_wd[widx % 2], 16)
            wait(TE, "t", s_wd[widx % 2], 16 * (widx // 2 + 1))
            wait(TE, "t", s_rle, k * NJC + q * 16 + 16)
            wait(TE, "t", s_v, m.get(("psdnfree", seq - 3), 0))
            for j in range(16):
                hslot = (q % 2) * 16 + j
                instr = TE.matmul(
                    ps_dn[:, (seq % 3) * 512:(seq % 3 + 1) * 512],
                    wdn[:, (widx % 2) * 2048 + j * 128:(widx % 2) * 2048 + (j + 1) * 128],
                    hid[:, hslot * TT:(hslot + 1) * TT],
                    start=(j == 0), stop=(j == 15))
            inc(instr, s_dng)
            m[("dnseq", widx)] = seq

        def emit_dne(k, q, dc):
            widx = (k * 4 + q) * 16 + dc
            seq = m[("dnseq", widx)]
            wait(VE, "v", s_dng, widx + 1)
            wait(VE, "v", s_v, m["dsosdone"])
            if q == 0:
                wait(VE, "v", s_s, m[("yi", k, dc)])
            wait(VE, "v", s_v, s_v.n)
            instr = VE.scalar_tensor_tensor(
                y_sb[:, dc * TT:(dc + 1) * TT],
                ps_dn[:, (seq % 3) * 512:(seq % 3 + 1) * 512],
                dsos[:, dc:dc + 1],
                y_sb[:, dc * TT:(dc + 1) * TT],
                ALU.mult, ALU.add)
            inc(instr, s_v)
            m[("psdnfree", seq)] = s_v.n

        def emit_wmul(k, s, p):
            gidx = (k * 32 + s) * 4 + p
            nh = s // 16
            hbase = ((2 * p) % 4) * TT
            cbase = (8 + 2 * p) * TT
            on_ve = (gidx % WMUL_VE_MOD) == WMUL_VE_MOD - 1
            eng = VE if on_ve else PO
            en = "v" if on_ve else "p"
            wait(eng, en, s_s, m[("bce", k, nh, 8 + 2 * p + 1)])
            wait(eng, en, s_v, m[("scan", k, s, 2 * p + 1)])
            if not on_ve:
                wait(PO, "p", s_v, m.get(("wfree", gidx - 2), 0))
            instr = eng.tensor_tensor(
                w_t[:, (gidx % 2) * 1024:(gidx % 2 + 1) * 1024],
                h_t[:, hbase:hbase + 2 * TT],
                bct[:, cbase:cbase + 2 * TT], ALU.mult)
            inc(instr, s_v if on_ve else s_p)
            m[("wm", k, s, p)] = (en, (s_v if on_ve else s_p).n)
            if p == 3:
                if on_ve:
                    m[("wmAllV", k, nh)] = s_v.n
                    m[("wmAllP", k, nh)] = m.get(("wmAllP", k, nh), 0)
                else:
                    m[("wmAllP", k, nh)] = s_p.n
                    m[("wmAllV", k, nh)] = m.get(("wmAllV", k, nh), 0)

        def emit_adds(k, s, plist):
            nh, dc = s // 16, s % 16
            for p in plist:
                gidx = (k * 32 + s) * 4 + p
                en, val = m[("wm", k, s, p)]
                if en == "p":
                    wait(VE, "v", s_p, val)
                else:
                    wait(VE, "v", s_v, val)
                w0 = w_t[:, (gidx % 2) * 1024:(gidx % 2) * 1024 + TT]
                w1 = w_t[:, (gidx % 2) * 1024 + TT:(gidx % 2 + 1) * 1024]
                if nh == 0:
                    if p == 0:
                        wait(VE, "v", s_s, m[("yi", k, dc)])
                    wait(VE, "v", s_v, s_v.n)
                    inc(VE.tensor_tensor(y_sb[:, dc * TT:(dc + 1) * TT],
                                         y_sb[:, dc * TT:(dc + 1) * TT], w0, ALU.add), s_v)
                    wait(VE, "v", s_v, s_v.n)
                    inc(VE.tensor_tensor(y_sb[:, dc * TT:(dc + 1) * TT],
                                         y_sb[:, dc * TT:(dc + 1) * TT], w1, ALU.add), s_v)
                else:
                    y2s = y2_sb[:, (dc % 2) * TT:(dc % 2 + 1) * TT]
                    if p == 0:
                        wait(VE, "v", s_v, s_v.n)
                        inc(VE.tensor_tensor(y2s, w0, w1, ALU.add), s_v)
                    else:
                        wait(VE, "v", s_v, s_v.n)
                        inc(VE.tensor_tensor(y2s, y2s, w0, ALU.add), s_v)
                        wait(VE, "v", s_v, s_v.n)
                        inc(VE.tensor_tensor(y2s, y2s, w1, ALU.add), s_v)
                m[("wfree", gidx)] = s_v.n
                if nh == 1 and p == 3:
                    wait(VE, "v", s_v, s_v.n)
                    inc(VE.tensor_tensor(y_sb[:, dc * TT:(dc + 1) * TT],
                                         y_sb[:, dc * TT:(dc + 1) * TT],
                                         y2s, ALU.add), s_v)
                    m[("merge", k, dc)] = s_v.n

        def emit_epilogue(k):
            for tsub in range(NSUB):
                xsl, xv = emit_xload(k * NSUB + tsub)
                for g in range(4):
                    j = tsub * 4 + g
                    seq = 80 * k + 64 + j
                    wait(TE, "t", s_v, m[("merge", k, 15)])
                    wait(TE, "t", s_v, m[("psdnfree", 80 * k + 63)])
                    wait(TE, "t", s_v, m.get(("psdnfree", seq - 3), 0))
                    for i in range(4):
                        dc = g * 4 + i
                        instr = TE.transpose(
                            ps_dn[:, (seq % 3) * 512 + i * 128:
                                  (seq % 3) * 512 + (i + 1) * 128],
                            y_sb[:, dc * TT + tsub * 128: dc * TT + (tsub + 1) * 128],
                            idf_sb[:, :])
                    inc(instr, s_t)
                    m[("epiT", k, j)] = s_t.n
                    oslot = g % 2
                    wait(VE, "v", s_t, m[("epiT", k, j)])
                    wait(VE, "v", s_x[xsl], xv)
                    wait(VE, "v", s_st[oslot], 16 * stcnt[oslot])
                    inc(VE.tensor_tensor(
                        out_t[:, oslot * 512:(oslot + 1) * 512],
                        ps_dn[:, (seq % 3) * 512:(seq % 3 + 1) * 512],
                        x_t[:, xsl * D + g * 512: xsl * D + (g + 1) * 512],
                        ALU.add), s_v)
                    m[("psdnfree", seq)] = s_v.n
                    wait(SY, "sy", s_v, s_v.n)
                    inc(SY.dma_start(
                        out_d[k * TT + tsub * 128: k * TT + (tsub + 1) * 128,
                              g * 512:(g + 1) * 512],
                        out_t[:, oslot * 512:(oslot + 1) * 512]), s_st[oslot], 16)
                    stcnt[oslot] += 1
                xq.append(("v", s_v.n))
            m[("epidone", k)] = s_v.n

        for k in range(NTT):
            if k == 0:
                for tsub in range(NSUB):
                    emit_stats(0, tsub)
            if k > 0:
                emit_epilogue(k - 1)

            for tsub in range(NSUB):
                xsl, xv = emit_xload(k * NSUB + tsub)
                c = tsub * 8
                wait(SC, "s", s_x[xsl], xv)
                wait(SC, "s", s_s, m[("stats", k, tsub)])
                wait(SC, "s", s_t, m.get(("xnT", k, tsub - 1), 0))
                inc(SC.activation(xnt[:, :], x_t[:, xsl * D:(xsl + 1) * D],
                                  AF.Identity, scale=ln8[:, c + 6:c + 7],
                                  bias=ln8[:, c + 7:c + 8]), s_s)
                m[("xa", k, tsub)] = s_s.n
                xq.append(("s", s_s.n))
                for g in range(4):
                    gseq = tsub * 4 + g
                    wait(TE, "t", s_s, m[("xa", k, tsub)])
                    wait(TE, "t", s_s, m.get(("xTe", k, gseq - 1),
                                             m.get(("xTeL", k - 1), 0)))
                    wait(TE, "t", s_cst, m["constld"])
                    if k > 0 and gseq == 0:
                        wait(TE, "t", s_upg, k * NJC)
                        wait(TE, "t", s_v, m[("uall", k - 1)])
                    for i in range(4):
                        dk = g * 4 + i
                        instr = TE.transpose(
                            ps_trb[:, i * 128:(i + 1) * 128],
                            xnt[:, dk * 128:(dk + 1) * 128], idb_sb[:, :])
                    inc(instr, s_t)
                    m[("xT", k, gseq)] = s_t.n
                    wait(SC, "s", s_t, m[("xT", k, gseq)])
                    src_ap = ps_trb[:, 0:512].rearrange(
                        "p (a b) -> p a b", a=4)
                    dst_ap = xn_d[:, :].rearrange("p (dk t) -> p dk t", dk=NDC)[
                        :, g * 4:(g + 1) * 4, tsub * 128:(tsub + 1) * 128]
                    inc(SC.copy(dst_ap, src_ap), s_s)
                    m[("xTe", k, gseq)] = s_s.n
                m[("xnT", k, tsub)] = s_t.n
            m[("xTeAll", k)] = s_s.n
            m[("xTeL", k)] = m[("xTe", k, 15)]

            wait(TE, "t", s_s, m[("xTeAll", k)])
            wait(TE, "t", s_s, m["wxcast"])
            wait(TE, "t", s_s, m.get(("spe", k - 1, 31), 0))
            for dk in range(NDC):
                instr = TE.matmul(ps_at[:, 0:512],
                                  wx_sb[:, dk * 160: dk * 160 + 128],
                                  xn_d[:, dk * TT:(dk + 1) * TT],
                                  start=(dk == 0), stop=(dk == NDC - 1))
            inc(instr, s_t); m[("xz", k)] = s_t.n
            for dk in range(NDC):
                instr = TE.matmul(ps_at[0:32, 512:1024],
                                  wx_sb[:, dk * 160 + 128: dk * 160 + 160],
                                  xn_d[:, dk * TT:(dk + 1) * TT],
                                  start=(dk == 0), stop=(dk == NDC - 1))
            inc(instr, s_t); m[("bcmm", k)] = s_t.n
            wait(SC, "s", s_t, m[("xz", k)])
            inc(SC.copy(dr_sb[:, :], ps_at[:, 0:512]), s_s)
            wait(SC, "s", s_t, m[("bcmm", k)])
            inc(SC.copy(bc_sb[:, :], ps_at[0:32, 512:1024]), s_s)
            m[("xze", k)] = s_s.n

            def emit_bcm(nh, i):
                pslot = i % 2
                wait(TE, "t", s_s, m[("xze", k)])
                wait(TE, "t", s_s, m.get(("bce", k, nh, i - 2), 0))
                if (k, nh) != (0, 0):
                    pk, pnh = (k, 0) if nh == 1 else (k - 1, 1)
                    if i < 8:
                        wait(TE, "t", s_p, m[("BuAll", pk, pnh)])
                    else:
                        wait(TE, "t", s_p, m[("wmAllP", pk, pnh)])
                        wait(TE, "t", s_v, m[("wmAllV", pk, pnh)])
                if nh == 1 and i < 2:
                    wait(TE, "t", s_s, m[("spe", k, 14 + i)])
                inc(TE.matmul(ps_at[:, pslot * 512:(pslot + 1) * 512],
                              sel_sb[nh][:, i * 128:(i + 1) * 128],
                              bc_sb[:, :], start=True, stop=True), s_t)
                m[("bcmT", k, nh, i)] = s_t.n
                wait(SC, "s", s_t, m[("bcmT", k, nh, i)])
                inc(SC.copy(bct[:, i * 512:(i + 1) * 512],
                            ps_at[:, pslot * 512:(pslot + 1) * 512]), s_s)
                m[("bce", k, nh, i)] = s_s.n

            for i in range(16):
                emit_bcm(0, i)

            if k % 4 == 0:
                wait(VE, "v", s_v, m.get(("h0done", k - 1), 0))
                inc(VE.memset(h0[:, :], 0.0), s_v)
                m[("h0ms", k)] = s_v.n

            for s in range(32):
                nh, dc = s // 16, s % 16
                pslot = dc % 2

                if s >= 1:
                    emit_adds(k, s - 1, [2])
                if nh == 1 and dc == 0:
                    emit_wmul(k, 15, 3)
                    emit_adds(k, 15, [3])
                    for i in range(16):
                        emit_bcm(1, i)

                wait(TE, "t", s_s, m[("xze", k)])
                wait(TE, "t", s_s, m[("bce", k, nh, 15)])
                wait(TE, "t", s_s, m.get(("spe", k, s - 2), 0))
                wait(TE, "t", s_s, m["wdtcast"])
                inc(TE.matmul(ps_at[:, pslot * 512:(pslot + 1) * 512],
                              wdt_sb[:, dc * 128:(dc + 1) * 128],
                              dr_sb[:, :], start=True, stop=True), s_t)
                m[("dmm", k, s)] = s_t.n

                if s >= 1:
                    for (q, ddc) in DNS_SCHED[s - 1]:
                        emit_dne(k, q, ddc)

                for jcg in UPS_SCHED[s]:
                    emit_upgroup(k, jcg)
                for (q, ddc) in DNS_SCHED[s]:
                    emit_downgroup(k, q, ddc)

                wait(SC, "s", s_t, m[("dmm", k, s)])
                if s >= 2:
                    wait(SC, "s", s_v, m[("uv", k, s - 2)])
                elif k > 0:
                    wait(SC, "s", s_v, m[("uv", k - 1, s + 30)])
                dslot = delta_t[:, pslot * TT:(pslot + 1) * TT]
                inc(SC.activation(dslot, ps_at[:, pslot * 512:(pslot + 1) * 512],
                                  AF.Exp, bias=bdt_sb[:, dc:dc + 1]), s_s)
                m[("spe", k, s)] = s_s.n
                wait(SC, "s", s_s, m[("spe", k, s)])
                inc(SC.activation(dslot, dslot, AF.Ln, bias=1.0), s_s)
                m[("sp", k, s)] = s_s.n

                if s < 8:
                    for yid in (2 * s, 2 * s + 1):
                        if k > 0:
                            wait(SC, "s", s_v, m[("epidone", k - 1)])
                        wait(SC, "s", s_s, m[("xTeAll", k)])
                        inc(SC.activation(y_sb[:, yid * TT:(yid + 1) * TT],
                                          xn_d[:, yid * TT:(yid + 1) * TT],
                                          AF.Identity,
                                          scale=dpar_sb[:, yid:yid + 1]), s_s)
                        m[("yi", k, yid)] = s_s.n

                wait(VE, "v", s_s, m[("sp", k, s)])
                if s >= 2:
                    wait(VE, "v", s_p, m[("BuStep", k, s - 2)])
                elif k > 0:
                    wait(VE, "v", s_p, m[("BuStep", k - 1, s + 30)])
                inc(VE.tensor_tensor(u_t[:, pslot * TT:(pslot + 1) * TT],
                                     dslot, xn_d[:, dc * TT:(dc + 1) * TT],
                                     ALU.mult), s_v)
                m[("uv", k, s)] = s_v.n
                if s == 31:
                    m[("uall", k)] = s_v.n

                for nl in range(8):
                    ncol = dc * N + nh * 8 + nl
                    aslot = nl % 2
                    hslot = nl % 4

                    if nl >= 2:
                        wait(SC, "s", s_v, m[("scan", k, s, nl - 2)])
                    elif s >= 1:
                        wait(SC, "s", s_v, m[("scan", k, s - 1, nl + 6)])
                    elif k > 0:
                        wait(SC, "s", s_v, m[("scan", k - 1, 31, nl + 6)])
                    wait(SC, "s", s_s, m[("sp", k, s)])
                    wait(SC, "s", s_s, m["negAdone"])
                    inc(SC.activation(dA_t[:, aslot * TT:(aslot + 1) * TT],
                                      dslot, AF.Exp,
                                      scale=negA[:, ncol:ncol + 1]), s_s)
                    m[("dA", k, s, nl)] = s_s.n

                    wait(PO, "p", s_v, m[("uv", k, s)])
                    wait(PO, "p", s_s, m[("bce", k, nh, nl)])
                    if nl >= 2:
                        wait(PO, "p", s_v, m[("scan", k, s, nl - 2)])
                    elif s >= 1:
                        wait(PO, "p", s_v, m[("scan", k, s - 1, nl + 6)])
                    inc(PO.tensor_tensor(Bu_t[:, aslot * TT:(aslot + 1) * TT],
                                         u_t[:, pslot * TT:(pslot + 1) * TT],
                                         bct[:, nl * TT:(nl + 1) * TT],
                                         ALU.mult), s_p)
                    m[("Bu", k, s, nl)] = s_p.n
                    if nl == 7:
                        m[("BuStep", k, s)] = s_p.n
                        m[("BuAll", k, nh)] = s_p.n

                    if nl == 1 and s >= 1 and s != 16:
                        emit_wmul(k, s - 1, 3)
                    if nl == 2 and s >= 1 and s != 16:
                        emit_adds(k, s - 1, [3])

                    wait(VE, "v", s_s, m[("dA", k, s, nl)])
                    wait(VE, "v", s_p, m[("Bu", k, s, nl)])
                    if k % 4 == 0:
                        wait(VE, "v", s_v, m[("h0ms", k)])
                    if nl < 4:
                        if s >= 1:
                            en, val = m[("wm", k, s - 1, (nl + 4) // 2)]
                            if en == "p":
                                wait(VE, "v", s_p, val)
                        elif k > 0:
                            en, val = m[("wm", k - 1, 31, (nl + 4) // 2)]
                            if en == "p":
                                wait(VE, "v", s_p, val)
                    else:
                        en, val = m[("wm", k, s, (nl - 4) // 2)]
                        if en == "p":
                            wait(VE, "v", s_p, val)
                    wait(VE, "v", s_v, m.get(("h0c", ncol), 0))
                    inc(VE.tensor_tensor_scan(
                        h_t[:, hslot * TT:(hslot + 1) * TT],
                        dA_t[:, aslot * TT:(aslot + 1) * TT],
                        Bu_t[:, aslot * TT:(aslot + 1) * TT],
                        h0[:, ncol:ncol + 1], ALU.mult, ALU.add), s_v)
                    m[("scan", k, s, nl)] = s_v.n
                    if nl % 2 == 1:
                        wait(VE, "v", s_v, m[("scan", k, s, nl)])
                        inc(VE.tensor_copy(h0[:, ncol - 1:ncol],
                                           h_t[:, (hslot - 1) * TT + TT - 1:
                                               (hslot - 1) * TT + TT]), s_v)
                        inc(VE.tensor_copy(h0[:, ncol:ncol + 1],
                                           h_t[:, hslot * TT + TT - 1:
                                               hslot * TT + TT]), s_v)
                        m[("h0c", ncol - 1)] = s_v.n - 1
                        m[("h0c", ncol)] = s_v.n

                    if nl in (3, 5, 7) and (nl - 3) // 2 < len(UPS_SCHED[s]):
                        emit_relu(k, UPS_SCHED[s][(nl - 3) // 2])
                    if nl in (3, 5, 7):
                        emit_wmul(k, s, (nl - 3) // 2)
                    if nl == 4:
                        emit_adds(k, s, [0])
                    if nl == 6:
                        emit_adds(k, s, [1])

                m[("h0done", k)] = s_v.n

                if 14 <= s <= 17 and k + 1 < NTT:
                    emit_stats(k + 1, s - 14)

                prep_pump(6)

            emit_wmul(k, 31, 3)
            emit_adds(k, 31, [2, 3])
            for (q, ddc) in DNS_SCHED[31]:
                emit_dne(k, q, ddc)

        emit_epilogue(NTT - 1)

        prep_pump_until(PREP_TOTAL)
        SY.wait_ge(s_st[0].h, 16 * stcnt[0])
        SY.wait_ge(s_st[1].h, 16 * stcnt[1])
    return nc


def _pack_inputs(W_xproj, W_dt, b_dt, A_log, D_param, up_w, dn_w, us, ds, osf):
    f32 = np.float32
    wx = np.ascontiguousarray(
        np.asarray(W_xproj, f32).T.reshape(16, 128, 160)
        .transpose(1, 0, 2).reshape(128, 16 * 160))
    wdt = np.ascontiguousarray(np.asarray(W_dt, f32).T)
    alog = np.ascontiguousarray(
        np.asarray(A_log, f32).reshape(16, 128, 16)
        .transpose(1, 0, 2).reshape(128, 256))
    bdt = np.ascontiguousarray(np.asarray(b_dt, f32).reshape(16, 128).T)
    dpar = np.ascontiguousarray(np.asarray(D_param, f32).reshape(16, 128).T)
    usc = np.ascontiguousarray(np.asarray(us, f32).reshape(64, 128).T)
    dsc = np.ascontiguousarray(np.asarray(ds, f32).reshape(16, 128).T)
    osc = np.ascontiguousarray(
        np.broadcast_to(np.asarray(osf, f32).reshape(1, 1), (128, 1)))
    upw = np.ascontiguousarray(
        np.asarray(up_w, f32).reshape(64, 128, 16, 128)
        .transpose(0, 3, 2, 1).reshape(64, 128, 2048))
    dnw = np.ascontiguousarray(
        np.asarray(dn_w, f32).reshape(16, 128, 2, 32, 128)
        .transpose(2, 0, 4, 3, 1).reshape(32, 128, 4096))
    identf = np.eye(128, dtype=f32)
    identb = np.eye(128, dtype=ml_dtypes.bfloat16)
    cvals = np.ascontiguousarray(
        np.broadcast_to(np.array([[1e-5, -0.5, -1.0]], f32), (128, 3)))
    sels = []
    for nh in range(2):
        sel = np.zeros((32, 16, 128), dtype=ml_dtypes.bfloat16)
        for i in range(16):
            row = (nh * 8 + i) if i < 8 else (16 + nh * 8 + (i - 8))
            sel[row, i, :] = 1.0
        sels.append(np.ascontiguousarray(sel.reshape(32, 2048)))
    return dict(wx=wx, wdt=wdt, alog=alog, bdt=bdt, dpar=dpar, usc=usc,
                dsc=dsc, osc=osc, upw=upw, dnw=dnw, identf=identf,
                identb=identb, sel0=sels[0], sel1=sels[1], cvals=cvals)


def kernel(L0_attn_x, L0_attn_W_xproj, L0_attn_W_dt, L0_attn_b_dt,
           L0_attn_A_log, L0_attn_D_param, L0_ffn_up_weight,
           L0_ffn_down_weight, L0_ffn_up_scale, L0_ffn_down_scale,
           L0_ffn_output_scale):
    shared = _pack_inputs(L0_attn_W_xproj, L0_attn_W_dt, L0_attn_b_dt,
                          L0_attn_A_log, L0_attn_D_param, L0_ffn_up_weight,
                          L0_ffn_down_weight, L0_ffn_up_scale,
                          L0_ffn_down_scale, L0_ffn_output_scale)
    x = np.asarray(L0_attn_x, np.float32)
    nc = build_nc()
    in_maps = []
    for c in range(8):
        im = dict(shared)
        im["x"] = np.ascontiguousarray(x[2 * c:2 * c + 2].reshape(B_CORE * T, D))
        in_maps.append(im)
    res = run_bass_kernel_spmd(nc, in_maps, core_ids=list(range(8)))
    out = np.stack([np.asarray(r["out"], np.float32).reshape(B_CORE, T, D)
                    for r in res.results], axis=0)
    return np.ascontiguousarray(out.reshape(16, T, D))


# version 07
# speedup vs baseline: 1.4747x; 1.0061x over previous
import numpy as np
import ml_dtypes
from contextlib import ExitStack

import concourse.bass as bass
import concourse.mybir as mybir
from concourse.bass_utils import run_bass_kernel_spmd

F32 = mybir.dt.float32
BF16 = mybir.dt.bfloat16
AF = mybir.ActivationFunctionType
ALU = mybir.AluOpType

B_CORE = 2
T = 2048
D = 2048
N = 16
HID = 8192
TT = 512
NTT = (B_CORE * T) // TT
NSUB = TT // 128
NDC = D // 128
NJC = HID // 128

WMUL_VE_MOD = 8

_QPAT = [3, 3, 3, 3, 2, 2]
UPS_SCHED = [[] for _ in range(33)]
DNS_SCHED = [[] for _ in range(33)]
for _q in range(4):
    _o = 0
    for _i, _c in enumerate(_QPAT):
        for _j in range(_c):
            UPS_SCHED[6 * _q + _i].append(_q * 16 + _o)
            DNS_SCHED[6 * _q + 6 + _i].append((_q, _o))
            _o += 1


class Sem:
    def __init__(self, stack, nc, name):
        self.h = stack.enter_context(nc.semaphore(name))
        self.name = name
        self.n = 0


def build_nc():
    nc = bass.Bass("TRN2", target_bir_lowering=False)

    x_d = nc.dram_tensor("x", [B_CORE * T, D], F32, kind="ExternalInput").ap()
    wx_d = nc.dram_tensor("wx", [128, NDC * 160], F32, kind="ExternalInput").ap()
    wdt_d = nc.dram_tensor("wdt", [128, D], F32, kind="ExternalInput").ap()
    alog_d = nc.dram_tensor("alog", [128, NDC * N], F32, kind="ExternalInput").ap()
    bdt_d = nc.dram_tensor("bdt", [128, NDC], F32, kind="ExternalInput").ap()
    dpar_d = nc.dram_tensor("dpar", [128, NDC], F32, kind="ExternalInput").ap()
    usc_d = nc.dram_tensor("usc", [128, NJC], F32, kind="ExternalInput").ap()
    dsc_d = nc.dram_tensor("dsc", [128, NDC], F32, kind="ExternalInput").ap()
    osc_d = nc.dram_tensor("osc", [128, 1], F32, kind="ExternalInput").ap()
    upw_d = nc.dram_tensor("upw", [NJC, 128, 2048], F32, kind="ExternalInput").ap()
    dnw_d = nc.dram_tensor("dnw", [32, 128, 4096], F32, kind="ExternalInput").ap()
    idf_d = nc.dram_tensor("identf", [128, 128], F32, kind="ExternalInput").ap()
    idb_d = nc.dram_tensor("identb", [128, 128], BF16, kind="ExternalInput").ap()
    sel0_d = nc.dram_tensor("sel0", [32, 2048], BF16, kind="ExternalInput").ap()
    sel1_d = nc.dram_tensor("sel1", [32, 2048], BF16, kind="ExternalInput").ap()
    cv_d = nc.dram_tensor("cvals", [128, 3], F32, kind="ExternalInput").ap()
    out_d = nc.dram_tensor("out", [B_CORE * T, D], F32, kind="ExternalOutput").ap()
    scr_up = nc.dram_tensor("scr_up", [NJC, 128, 2048], BF16, kind="Internal").ap()
    scr_dn = nc.dram_tensor("scr_dn", [32, 128, 4096], BF16, kind="Internal").ap()

    stk = ExitStack()
    sb = lambda name, shape, dt: stk.enter_context(nc.sbuf_tensor(name, shape, dt))
    pst = lambda name, shape, dt: stk.enter_context(nc.psum_tensor(name, shape, dt))
    with stk:
        xn_d = sb("xn_d", [128, NDC * TT], BF16)
        y_sb = sb("y", [128, NDC * TT], F32)
        x_t = sb("x_t", [128, 2 * D], F32)
        xnt = sb("xnt", [128, D], BF16)
        junk = xnt
        ln8 = sb("ln8", [128, 32], F32)
        bct = sb("bct", [128, 16 * TT], BF16)
        dr_sb = sb("dr", [128, TT], BF16)
        bc_sb = sb("bc", [32, TT], BF16)
        sel_sb = [sb("sel0s", [32, 2048], BF16), sb("sel1s", [32, 2048], BF16)]
        delta_t = sb("delta", [128, 2 * TT], F32)
        u_t = sb("u", [128, 2 * TT], F32)
        dA_t = sb("dA", [128, 2 * TT], F32)
        Bu_t = sb("Bu", [128, 2 * TT], F32)
        h_t = sb("h", [128, 4 * TT], F32)
        w_t = sb("w", [128, 2 * 2 * TT], F32)
        y2_sb = sb("y2", [128, 2 * TT], F32)
        h0 = sb("h0", [128, NDC * N], F32)
        hid = sb("hid", [128, 32 * TT], BF16)
        wup = sb("wup", [128, 2 * 2048], BF16)
        wdn = sb("wdn", [128, 2 * 2048], BF16)
        out_t = sb("out_t", [128, 2 * 512], F32)
        wx_sb = sb("wx_sb", [128, NDC * 160], BF16)
        wdt_sb = sb("wdt_sb", [128, D], BF16)
        negA = sb("negA", [128, NDC * N], F32)
        bdt_sb = sb("bdt_sb", [128, NDC], F32)
        dpar_sb = sb("dpar_sb", [128, NDC], F32)
        usc_sb = sb("usc_sb", [128, NJC], F32)
        dsos = sb("dsos", [128, NDC], F32)
        osc_sb = sb("osc_sb", [128, 1], F32)
        cv_sb = sb("cv_sb", [128, 3], F32)
        idf_sb = sb("idf_sb", [128, 128], F32)
        idb_sb = sb("idb_sb", [128, 128], BF16)
        stg = sb("stg", [128, 2048], F32)
        sgn = sb("sgn", [128, 2 * 1024], BF16)

        ps_trb = pst("ps_trb", [128, 1024], BF16)
        ps_at = pst("ps_at", [128, 1024], F32)
        ps_hid = pst("ps_hid", [128, 1024], F32)
        ps_dn = pst("ps_dn", [128, 1536], F32)

        s_cst = Sem(stk, nc, "s_cst")
        s_pl = [Sem(stk, nc, "s_pl0"), Sem(stk, nc, "s_pl1")]
        s_sw = [Sem(stk, nc, "s_sw0"), Sem(stk, nc, "s_sw1")]
        s_x = [Sem(stk, nc, "s_x0"), Sem(stk, nc, "s_x1")]
        s_wu = [Sem(stk, nc, "s_wu0"), Sem(stk, nc, "s_wu1")]
        s_wd = [Sem(stk, nc, "s_wd0"), Sem(stk, nc, "s_wd1")]
        s_st = [Sem(stk, nc, "s_st0"), Sem(stk, nc, "s_st1")]
        s_t = Sem(stk, nc, "s_t")
        s_upg = Sem(stk, nc, "s_upg")
        s_dng = Sem(stk, nc, "s_dng")
        s_s = Sem(stk, nc, "s_s")
        s_rle = Sem(stk, nc, "s_rle")
        s_v = Sem(stk, nc, "s_v")
        s_p = Sem(stk, nc, "s_p")

        m = {}
        W = {}

        def wait(eng, ename, s, v):
            if not v or v <= 0:
                return
            key = (ename, s.name)
            if W.get(key, 0) >= v:
                return
            eng.wait_ge(s.h, v)
            W[key] = v

        def inc(instr, s, k=1):
            instr.then_inc(s.h, k)
            s.n += k
            return instr

        TE, SC, VE, PO, SY = nc.tensor, nc.scalar, nc.vector, nc.gpsimd, nc.sync

        inc(SY.dma_start(stg[:, 0:1280], wx_d[:, 0:1280]), s_cst, 16)
        inc(SY.dma_start(stg[:, 1280:2048], wx_d[:, 1280:2048]), s_cst, 16)
        inc(SY.dma_start(negA[:, :], alog_d[:, :]), s_cst, 16)
        inc(SY.dma_start(bdt_sb[:, :], bdt_d[:, :]), s_cst, 16)
        inc(SY.dma_start(dpar_sb[:, :], dpar_d[:, :]), s_cst, 16)
        inc(SY.dma_start(usc_sb[:, :], usc_d[:, :]), s_cst, 16)
        inc(SY.dma_start(dsos[:, :], dsc_d[:, :]), s_cst, 16)
        inc(SY.dma_start(osc_sb[:, :], osc_d[:, :]), s_cst, 16)
        inc(SY.dma_start(cv_sb[:, :], cv_d[:, :]), s_cst, 16)
        inc(SY.dma_start(idf_sb[:, :], idf_d[:, :]), s_cst, 16)
        inc(SY.dma_start(idb_sb[:, :], idb_d[:, :]), s_cst, 16)
        inc(SY.dma_start(sel_sb[0][:, :], sel0_d[:, :]), s_cst, 16)
        inc(SY.dma_start(sel_sb[1][:, :], sel1_d[:, :]), s_cst, 16)
        m["constld"] = s_cst.n

        SC.add_instruction(mybir.InstLoadActFuncSet(
            name=nc.get_next_instruction_name(), act_func_set_id=6, ins=[], outs=[]))

        wait(SC, "s", s_cst, m["constld"])
        inc(SC.copy(wx_sb[:, 0:1280], stg[:, 0:1280]), s_s)
        inc(SC.copy(wx_sb[:, 1280:2048], stg[:, 1280:2048]), s_s); m["wxcast0"] = s_s.n
        inc(SC.activation(negA[:, :], negA[:, :], AF.Exp), s_s); m["negAe"] = s_s.n
        wait(SC, "s", s_s, m["negAe"])
        inc(SC.mul(negA[:, :], negA[:, :], cv_sb[:, 2:3]), s_s); m["negAdone"] = s_s.n
        wait(VE, "v", s_cst, m["constld"])
        inc(VE.tensor_scalar(dsos[:, :], dsos[:, :], osc_sb[:, 0:1], None, ALU.mult), s_v)
        m["dsosdone"] = s_v.n
        wait(SY, "sy", s_s, m["wxcast0"])
        inc(SY.dma_start(stg[:, 0:512], wx_d[:, 2048:2560]), s_cst, 16); m["wxld2"] = s_cst.n
        wait(SC, "s", s_cst, m["wxld2"])
        inc(SC.copy(wx_sb[:, 2048:2560], stg[:, 0:512]), s_s); m["wxcast"] = s_s.n
        wait(SY, "sy", s_s, m["wxcast"])
        inc(SY.dma_start(stg[:, 0:2048], wdt_d[:, :]), s_cst, 16); m["wdtld"] = s_cst.n
        wait(SC, "s", s_cst, m["wdtld"])
        inc(SC.copy(wdt_sb[:, :], stg[:, 0:2048]), s_s); m["wdtcast"] = s_s.n

        PREP_TOTAL = 256
        prep = {"sg": 0, "ld": 0}
        pm_sign = {}

        def prep_src(p):
            g, r = p // 32, p % 32
            if g % 2 == 0:
                jcg = (g // 2) * 16 + r // 2
                c = r % 2
                return (upw_d[jcg, :, c * 1024:(c + 1) * 1024],
                        scr_up[jcg, :, c * 1024:(c + 1) * 1024])
            qidx = g // 2
            h2, ch = qidx // 2, qidx % 2
            dc = r // 2
            ci = ch * 2 + r % 2
            return (dnw_d[h2 * 16 + dc, :, ci * 1024:(ci + 1) * 1024],
                    scr_dn[h2 * 16 + dc, :, ci * 1024:(ci + 1) * 1024])

        def _prep_load(i):
            src, _ = prep_src(i)
            slot = i % 2
            inc(SC.dma_start(stg[:, slot * 1024:(slot + 1) * 1024], src),
                s_pl[slot], 16)

        def prep_pump(nchunks):
            for _ in range(nchunks):
                i = prep["sg"]
                if i >= PREP_TOTAL:
                    return
                while prep["ld"] < min(i + 2, PREP_TOTAL):
                    _prep_load(prep["ld"])
                    prep["ld"] += 1
                prep["sg"] = i + 1
                _, dst = prep_src(i)
                slot = i % 2
                wait(SC, "s", s_pl[slot], 16 * (i // 2 + 1))
                wait(SC, "s", s_sw[slot], 16 * (i // 2))
                inc(SC.sign(sgn[:, slot * 1024:(slot + 1) * 1024],
                            stg[:, slot * 1024:(slot + 1) * 1024]), s_s)
                pm_sign[i] = s_s.n
                wait(SC, "s", s_s, pm_sign[i])
                inc(SC.dma_start(dst, sgn[:, slot * 1024:(slot + 1) * 1024]),
                    s_sw[slot], 16)

        def prep_pump_until(n):
            while prep["sg"] < n:
                prep_pump(1)

        def prep_ready(eng, ename, p_end):
            wait(eng, ename, s_sw[0], 16 * ((p_end + 1) // 2))
            wait(eng, ename, s_sw[1], 16 * (p_end // 2))

        xq = []
        xtot = [0]
        xcnt = [0, 0]
        stcnt = [0, 0]

        def emit_xload(rows):
            i = xtot[0]
            slot = i % 2
            if i >= 2:
                kind, val = xq[i - 2]
                if kind == "s":
                    wait(SY, "sy", s_s, val)
                else:
                    wait(SY, "sy", s_v, val)
            inc(SY.dma_start(x_t[:, slot * D:(slot + 1) * D],
                             x_d[rows * 128:(rows + 1) * 128, :]),
                s_x[slot], 16)
            xtot[0] = i + 1
            xcnt[slot] += 1
            return slot, 16 * xcnt[slot]

        def emit_stats(k, tsub):
            xsl, xv = emit_xload(k * NSUB + tsub)
            xs = x_t[:, xsl * D:(xsl + 1) * D]
            c = tsub * 8
            wait(SC, "s", s_x[xsl], xv)
            inc(SC.activation(junk[:, :], xs, AF.Square, accum_out=ln8[:, c:c + 1]), s_s)
            wait(SC, "s", s_s, s_s.n)
            inc(SC.activation(junk[:, :], xs, AF.Copy, accum_out=ln8[:, c + 1:c + 2]), s_s)
            m[("lnacc", k, tsub)] = s_s.n
            xq.append(("s", s_s.n))
            wait(VE, "v", s_s, m[("lnacc", k, tsub)])
            inc(VE.tensor_scalar(ln8[:, c + 2:c + 3], ln8[:, c + 1:c + 2],
                                 -1.0 / D, None, ALU.mult), s_v)
            inc(VE.tensor_scalar(ln8[:, c + 3:c + 4], ln8[:, c:c + 1],
                                 1.0 / D, None, ALU.mult), s_v)
            m[("lnv1", k, tsub)] = s_v.n
            wait(SC, "s", s_v, m[("lnv1", k, tsub)])
            inc(SC.activation(ln8[:, c + 4:c + 5], ln8[:, c + 2:c + 3], AF.Square), s_s)
            m[("m2", k, tsub)] = s_s.n
            wait(VE, "v", s_s, m[("m2", k, tsub)])
            wait(VE, "v", s_v, m[("lnv1", k, tsub)])
            inc(VE.tensor_tensor(ln8[:, c + 3:c + 4], ln8[:, c + 3:c + 4],
                                 ln8[:, c + 4:c + 5], ALU.subtract), s_v)
            m[("lnv", k, tsub)] = s_v.n
            wait(SC, "s", s_v, m[("lnv", k, tsub)])
            inc(SC.activation(ln8[:, c + 5:c + 6], ln8[:, c + 3:c + 4], AF.Ln,
                              bias=cv_sb[:, 0:1]), s_s)
            wait(SC, "s", s_s, s_s.n)
            inc(SC.activation(ln8[:, c + 6:c + 7], ln8[:, c + 5:c + 6], AF.Exp,
                              scale=cv_sb[:, 1:2]), s_s)
            wait(SC, "s", s_s, s_s.n)
            inc(SC.activation(ln8[:, c + 7:c + 8], ln8[:, c + 2:c + 3], AF.Identity,
                              scale=ln8[:, c + 6:c + 7]), s_s)
            m[("stats", k, tsub)] = s_s.n

        def emit_upgroup(k, jcg):
            gidx = k * NJC + jcg
            prep_pump_until((jcg // 16) * 64 + (jcg % 16 + 1) * 2)
            wait(SY, "sy", s_upg, gidx - 1)
            prep_ready(SY, "sy", (jcg // 16) * 64 + (jcg % 16 + 1) * 2)
            inc(SY.dma_start(wup[:, (gidx % 2) * 2048:(gidx % 2 + 1) * 2048],
                             scr_up[jcg, :, :]), s_wu[gidx % 2], 16)
            wait(TE, "t", s_wu[gidx % 2], 16 * (gidx // 2 + 1))
            wait(TE, "t", s_rle, gidx - 1)
            wait(TE, "t", s_s, m[("xTeAll", k)])
            for dk in range(NDC):
                instr = TE.matmul(
                    ps_hid[:, (gidx % 2) * 512:(gidx % 2 + 1) * 512],
                    wup[:, (gidx % 2) * 2048 + dk * 128:(gidx % 2) * 2048 + (dk + 1) * 128],
                    xn_d[:, dk * TT:(dk + 1) * TT],
                    start=(dk == 0), stop=(dk == NDC - 1))
            inc(instr, s_upg)

        def emit_relu(k, jcg):
            gidx = k * NJC + jcg
            q = jcg // 16
            wait(SC, "s", s_upg, gidx + 1)
            if q >= 2:
                wait(SC, "s", s_dng, (k * 4 + q - 2) * 16 + 16)
            elif k > 0:
                wait(SC, "s", s_dng, ((k - 1) * 4 + q + 2) * 16 + 16)
            hslot = (q % 2) * 16 + (jcg % 16)
            inc(SC.activation(hid[:, hslot * TT:(hslot + 1) * TT],
                              ps_hid[:, (gidx % 2) * 512:(gidx % 2 + 1) * 512],
                              AF.Relu, scale=usc_sb[:, jcg:jcg + 1]), s_rle)

        def emit_downgroup(k, q, dc):
            widx = (k * 4 + q) * 16 + dc
            seq = 80 * k + q * 16 + dc
            prep_pump_until(q * 64 + 32 + (dc + 1) * 2)
            wait(SY, "sy", s_dng, widx - 1)
            prep_ready(SY, "sy", q * 64 + 32 + (dc + 1) * 2)
            h2, ch = q // 2, q % 2
            inc(SY.dma_start(wdn[:, (widx % 2) * 2048:(widx % 2 + 1) * 2048],
                             scr_dn[h2 * 16 + dc, :, ch * 2048:(ch + 1) * 2048]),
                s_wd[widx % 2], 16)
            wait(TE, "t", s_wd[widx % 2], 16 * (widx // 2 + 1))
            wait(TE, "t", s_rle, k * NJC + q * 16 + 16)
            wait(TE, "t", s_v, m.get(("psdnfree", seq - 3), 0))
            for j in range(16):
                hslot = (q % 2) * 16 + j
                instr = TE.matmul(
                    ps_dn[:, (seq % 3) * 512:(seq % 3 + 1) * 512],
                    wdn[:, (widx % 2) * 2048 + j * 128:(widx % 2) * 2048 + (j + 1) * 128],
                    hid[:, hslot * TT:(hslot + 1) * TT],
                    start=(j == 0), stop=(j == 15))
            inc(instr, s_dng)
            m[("dnseq", widx)] = seq

        def emit_dne(k, q, dc):
            widx = (k * 4 + q) * 16 + dc
            seq = m[("dnseq", widx)]
            wait(VE, "v", s_dng, widx + 1)
            wait(VE, "v", s_v, m["dsosdone"])
            if q == 0:
                wait(VE, "v", s_s, m[("yi", k, dc)])
            wait(VE, "v", s_v, s_v.n)
            instr = VE.scalar_tensor_tensor(
                y_sb[:, dc * TT:(dc + 1) * TT],
                ps_dn[:, (seq % 3) * 512:(seq % 3 + 1) * 512],
                dsos[:, dc:dc + 1],
                y_sb[:, dc * TT:(dc + 1) * TT],
                ALU.mult, ALU.add)
            inc(instr, s_v)
            m[("psdnfree", seq)] = s_v.n

        def emit_wmul(k, s, p):
            gidx = (k * 32 + s) * 4 + p
            nh = s // 16
            hbase = ((2 * p) % 4) * TT
            cbase = (8 + 2 * p) * TT
            on_ve = (gidx % WMUL_VE_MOD) == WMUL_VE_MOD - 1
            eng = VE if on_ve else PO
            en = "v" if on_ve else "p"
            wait(eng, en, s_s, m[("bce", k, nh, 8 + 2 * p + 1)])
            wait(eng, en, s_v, m[("scan", k, s, 2 * p + 1)])
            if not on_ve:
                wait(PO, "p", s_v, m.get(("wfree", gidx - 2), 0))
            instr = eng.tensor_tensor(
                w_t[:, (gidx % 2) * 1024:(gidx % 2 + 1) * 1024],
                h_t[:, hbase:hbase + 2 * TT],
                bct[:, cbase:cbase + 2 * TT], ALU.mult)
            inc(instr, s_v if on_ve else s_p)
            m[("wm", k, s, p)] = (en, (s_v if on_ve else s_p).n)
            if p == 3:
                if on_ve:
                    m[("wmAllV", k, nh)] = s_v.n
                    m[("wmAllP", k, nh)] = m.get(("wmAllP", k, nh), 0)
                else:
                    m[("wmAllP", k, nh)] = s_p.n
                    m[("wmAllV", k, nh)] = m.get(("wmAllV", k, nh), 0)

        def emit_adds(k, s, plist):
            nh, dc = s // 16, s % 16
            for p in plist:
                gidx = (k * 32 + s) * 4 + p
                en, val = m[("wm", k, s, p)]
                if en == "p":
                    wait(VE, "v", s_p, val)
                else:
                    wait(VE, "v", s_v, val)
                w0 = w_t[:, (gidx % 2) * 1024:(gidx % 2) * 1024 + TT]
                w1 = w_t[:, (gidx % 2) * 1024 + TT:(gidx % 2 + 1) * 1024]
                if nh == 0:
                    if p == 0:
                        wait(VE, "v", s_s, m[("yi", k, dc)])
                    inc(VE.tensor_tensor(w0, w0, w1, ALU.add), s_v)
                    wait(VE, "v", s_v, s_v.n)
                    inc(VE.tensor_tensor(y_sb[:, dc * TT:(dc + 1) * TT],
                                         y_sb[:, dc * TT:(dc + 1) * TT], w0, ALU.add), s_v)
                else:
                    y2s = y2_sb[:, (dc % 2) * TT:(dc % 2 + 1) * TT]
                    if p == 0:
                        wait(VE, "v", s_v, s_v.n)
                        inc(VE.tensor_tensor(y2s, w0, w1, ALU.add), s_v)
                    else:
                        inc(VE.tensor_tensor(w0, w0, w1, ALU.add), s_v)
                        wait(VE, "v", s_v, s_v.n)
                        inc(VE.tensor_tensor(y2s, y2s, w0, ALU.add), s_v)
                m[("wfree", gidx)] = s_v.n
                if nh == 1 and p == 3:
                    wait(VE, "v", s_v, s_v.n)
                    inc(VE.tensor_tensor(y_sb[:, dc * TT:(dc + 1) * TT],
                                         y_sb[:, dc * TT:(dc + 1) * TT],
                                         y2s, ALU.add), s_v)
                    m[("merge", k, dc)] = s_v.n

        def emit_epilogue(k):
            for tsub in range(NSUB):
                xsl, xv = emit_xload(k * NSUB + tsub)
                for g in range(4):
                    j = tsub * 4 + g
                    seq = 80 * k + 64 + j
                    wait(TE, "t", s_v, m[("merge", k, 15)])
                    wait(TE, "t", s_v, m[("psdnfree", 80 * k + 63)])
                    wait(TE, "t", s_v, m.get(("psdnfree", seq - 3), 0))
                    for i in range(4):
                        dc = g * 4 + i
                        instr = TE.transpose(
                            ps_dn[:, (seq % 3) * 512 + i * 128:
                                  (seq % 3) * 512 + (i + 1) * 128],
                            y_sb[:, dc * TT + tsub * 128: dc * TT + (tsub + 1) * 128],
                            idf_sb[:, :])
                    inc(instr, s_t)
                    m[("epiT", k, j)] = s_t.n
                    oslot = g % 2
                    wait(VE, "v", s_t, m[("epiT", k, j)])
                    wait(VE, "v", s_x[xsl], xv)
                    wait(VE, "v", s_st[oslot], 16 * stcnt[oslot])
                    inc(VE.tensor_tensor(
                        out_t[:, oslot * 512:(oslot + 1) * 512],
                        ps_dn[:, (seq % 3) * 512:(seq % 3 + 1) * 512],
                        x_t[:, xsl * D + g * 512: xsl * D + (g + 1) * 512],
                        ALU.add), s_v)
                    m[("psdnfree", seq)] = s_v.n
                    wait(SY, "sy", s_v, s_v.n)
                    inc(SY.dma_start(
                        out_d[k * TT + tsub * 128: k * TT + (tsub + 1) * 128,
                              g * 512:(g + 1) * 512],
                        out_t[:, oslot * 512:(oslot + 1) * 512]), s_st[oslot], 16)
                    stcnt[oslot] += 1
                xq.append(("v", s_v.n))
            m[("epidone", k)] = s_v.n

        for k in range(NTT):
            if k == 0:
                for tsub in range(NSUB):
                    emit_stats(0, tsub)
            if k > 0:
                emit_epilogue(k - 1)

            for tsub in range(NSUB):
                xsl, xv = emit_xload(k * NSUB + tsub)
                c = tsub * 8
                wait(SC, "s", s_x[xsl], xv)
                wait(SC, "s", s_s, m[("stats", k, tsub)])
                wait(SC, "s", s_t, m.get(("xnT", k, tsub - 1), 0))
                inc(SC.activation(xnt[:, :], x_t[:, xsl * D:(xsl + 1) * D],
                                  AF.Identity, scale=ln8[:, c + 6:c + 7],
                                  bias=ln8[:, c + 7:c + 8]), s_s)
                m[("xa", k, tsub)] = s_s.n
                xq.append(("s", s_s.n))
                for g in range(4):
                    gseq = tsub * 4 + g
                    wait(TE, "t", s_s, m[("xa", k, tsub)])
                    wait(TE, "t", s_s, m.get(("xTe", k, gseq - 1),
                                             m.get(("xTeL", k - 1), 0)))
                    wait(TE, "t", s_cst, m["constld"])
                    if k > 0 and gseq == 0:
                        wait(TE, "t", s_upg, k * NJC)
                        wait(TE, "t", s_v, m[("uall", k - 1)])
                    for i in range(4):
                        dk = g * 4 + i
                        instr = TE.transpose(
                            ps_trb[:, i * 128:(i + 1) * 128],
                            xnt[:, dk * 128:(dk + 1) * 128], idb_sb[:, :])
                    inc(instr, s_t)
                    m[("xT", k, gseq)] = s_t.n
                    wait(SC, "s", s_t, m[("xT", k, gseq)])
                    src_ap = ps_trb[:, 0:512].rearrange(
                        "p (a b) -> p a b", a=4)
                    dst_ap = xn_d[:, :].rearrange("p (dk t) -> p dk t", dk=NDC)[
                        :, g * 4:(g + 1) * 4, tsub * 128:(tsub + 1) * 128]
                    inc(SC.copy(dst_ap, src_ap), s_s)
                    m[("xTe", k, gseq)] = s_s.n
                m[("xnT", k, tsub)] = s_t.n
            m[("xTeAll", k)] = s_s.n
            m[("xTeL", k)] = m[("xTe", k, 15)]

            wait(TE, "t", s_s, m[("xTeAll", k)])
            wait(TE, "t", s_s, m["wxcast"])
            wait(TE, "t", s_s, m.get(("spe", k - 1, 31), 0))
            for dk in range(NDC):
                instr = TE.matmul(ps_at[:, 0:512],
                                  wx_sb[:, dk * 160: dk * 160 + 128],
                                  xn_d[:, dk * TT:(dk + 1) * TT],
                                  start=(dk == 0), stop=(dk == NDC - 1))
            inc(instr, s_t); m[("xz", k)] = s_t.n
            for dk in range(NDC):
                instr = TE.matmul(ps_at[0:32, 512:1024],
                                  wx_sb[:, dk * 160 + 128: dk * 160 + 160],
                                  xn_d[:, dk * TT:(dk + 1) * TT],
                                  start=(dk == 0), stop=(dk == NDC - 1))
            inc(instr, s_t); m[("bcmm", k)] = s_t.n
            wait(SC, "s", s_t, m[("xz", k)])
            inc(SC.copy(dr_sb[:, :], ps_at[:, 0:512]), s_s)
            wait(SC, "s", s_t, m[("bcmm", k)])
            inc(SC.copy(bc_sb[:, :], ps_at[0:32, 512:1024]), s_s)
            m[("xze", k)] = s_s.n

            def emit_bcm(nh, i):
                pslot = i % 2
                wait(TE, "t", s_s, m[("xze", k)])
                wait(TE, "t", s_s, m.get(("bce", k, nh, i - 2), 0))
                if (k, nh) != (0, 0):
                    pk, pnh = (k, 0) if nh == 1 else (k - 1, 1)
                    if i < 8:
                        wait(TE, "t", s_p, m[("BuAll", pk, pnh)])
                    else:
                        wait(TE, "t", s_p, m[("wmAllP", pk, pnh)])
                        wait(TE, "t", s_v, m[("wmAllV", pk, pnh)])
                if nh == 1 and i < 2:
                    wait(TE, "t", s_s, m[("spe", k, 14 + i)])
                inc(TE.matmul(ps_at[:, pslot * 512:(pslot + 1) * 512],
                              sel_sb[nh][:, i * 128:(i + 1) * 128],
                              bc_sb[:, :], start=True, stop=True), s_t)
                m[("bcmT", k, nh, i)] = s_t.n
                wait(SC, "s", s_t, m[("bcmT", k, nh, i)])
                inc(SC.copy(bct[:, i * 512:(i + 1) * 512],
                            ps_at[:, pslot * 512:(pslot + 1) * 512]), s_s)
                m[("bce", k, nh, i)] = s_s.n

            for i in range(16):
                emit_bcm(0, i)

            if k % 4 == 0:
                wait(VE, "v", s_v, m.get(("h0done", k - 1), 0))
                inc(VE.memset(h0[:, :], 0.0), s_v)
                m[("h0ms", k)] = s_v.n

            for s in range(32):
                nh, dc = s // 16, s % 16
                pslot = dc % 2

                if s >= 1:
                    emit_adds(k, s - 1, [2])
                if nh == 1 and dc == 0:
                    emit_wmul(k, 15, 3)
                    emit_adds(k, 15, [3])
                    for i in range(16):
                        emit_bcm(1, i)

                wait(TE, "t", s_s, m[("xze", k)])
                wait(TE, "t", s_s, m[("bce", k, nh, 15)])
                wait(TE, "t", s_s, m.get(("spe", k, s - 2), 0))
                wait(TE, "t", s_s, m["wdtcast"])
                inc(TE.matmul(ps_at[:, pslot * 512:(pslot + 1) * 512],
                              wdt_sb[:, dc * 128:(dc + 1) * 128],
                              dr_sb[:, :], start=True, stop=True), s_t)
                m[("dmm", k, s)] = s_t.n

                if s >= 1:
                    for (q, ddc) in DNS_SCHED[s - 1]:
                        emit_dne(k, q, ddc)

                for jcg in UPS_SCHED[s]:
                    emit_upgroup(k, jcg)
                for (q, ddc) in DNS_SCHED[s]:
                    emit_downgroup(k, q, ddc)

                wait(SC, "s", s_t, m[("dmm", k, s)])
                if s >= 2:
                    wait(SC, "s", s_v, m[("uv", k, s - 2)])
                elif k > 0:
                    wait(SC, "s", s_v, m[("uv", k - 1, s + 30)])
                dslot = delta_t[:, pslot * TT:(pslot + 1) * TT]
                inc(SC.activation(dslot, ps_at[:, pslot * 512:(pslot + 1) * 512],
                                  AF.Exp, bias=bdt_sb[:, dc:dc + 1]), s_s)
                m[("spe", k, s)] = s_s.n
                wait(SC, "s", s_s, m[("spe", k, s)])
                inc(SC.activation(dslot, dslot, AF.Ln, bias=1.0), s_s)
                m[("sp", k, s)] = s_s.n

                if s < 8:
                    for yid in (2 * s, 2 * s + 1):
                        if k > 0:
                            wait(SC, "s", s_v, m[("epidone", k - 1)])
                        wait(SC, "s", s_s, m[("xTeAll", k)])
                        inc(SC.activation(y_sb[:, yid * TT:(yid + 1) * TT],
                                          xn_d[:, yid * TT:(yid + 1) * TT],
                                          AF.Identity,
                                          scale=dpar_sb[:, yid:yid + 1]), s_s)
                        m[("yi", k, yid)] = s_s.n

                wait(VE, "v", s_s, m[("sp", k, s)])
                if s >= 2:
                    wait(VE, "v", s_p, m[("BuStep", k, s - 2)])
                elif k > 0:
                    wait(VE, "v", s_p, m[("BuStep", k - 1, s + 30)])
                inc(VE.tensor_tensor(u_t[:, pslot * TT:(pslot + 1) * TT],
                                     dslot, xn_d[:, dc * TT:(dc + 1) * TT],
                                     ALU.mult), s_v)
                m[("uv", k, s)] = s_v.n
                if s == 31:
                    m[("uall", k)] = s_v.n

                for nl in range(8):
                    ncol = dc * N + nh * 8 + nl
                    aslot = nl % 2
                    hslot = nl % 4

                    if nl >= 2:
                        wait(SC, "s", s_v, m[("scan", k, s, nl - 2)])
                    elif s >= 1:
                        wait(SC, "s", s_v, m[("scan", k, s - 1, nl + 6)])
                    elif k > 0:
                        wait(SC, "s", s_v, m[("scan", k - 1, 31, nl + 6)])
                    wait(SC, "s", s_s, m[("sp", k, s)])
                    wait(SC, "s", s_s, m["negAdone"])
                    inc(SC.activation(dA_t[:, aslot * TT:(aslot + 1) * TT],
                                      dslot, AF.Exp,
                                      scale=negA[:, ncol:ncol + 1]), s_s)
                    m[("dA", k, s, nl)] = s_s.n

                    wait(PO, "p", s_v, m[("uv", k, s)])
                    wait(PO, "p", s_s, m[("bce", k, nh, nl)])
                    if nl >= 2:
                        wait(PO, "p", s_v, m[("scan", k, s, nl - 2)])
                    elif s >= 1:
                        wait(PO, "p", s_v, m[("scan", k, s - 1, nl + 6)])
                    inc(PO.tensor_tensor(Bu_t[:, aslot * TT:(aslot + 1) * TT],
                                         u_t[:, pslot * TT:(pslot + 1) * TT],
                                         bct[:, nl * TT:(nl + 1) * TT],
                                         ALU.mult), s_p)
                    m[("Bu", k, s, nl)] = s_p.n
                    if nl == 7:
                        m[("BuStep", k, s)] = s_p.n
                        m[("BuAll", k, nh)] = s_p.n

                    if nl == 1 and s >= 1 and s != 16:
                        emit_wmul(k, s - 1, 3)
                    if nl == 2 and s >= 1 and s != 16:
                        emit_adds(k, s - 1, [3])

                    wait(VE, "v", s_s, m[("dA", k, s, nl)])
                    wait(VE, "v", s_p, m[("Bu", k, s, nl)])
                    if k % 4 == 0:
                        wait(VE, "v", s_v, m[("h0ms", k)])
                    if nl < 4:
                        if s >= 1:
                            en, val = m[("wm", k, s - 1, (nl + 4) // 2)]
                            if en == "p":
                                wait(VE, "v", s_p, val)
                        elif k > 0:
                            en, val = m[("wm", k - 1, 31, (nl + 4) // 2)]
                            if en == "p":
                                wait(VE, "v", s_p, val)
                    else:
                        en, val = m[("wm", k, s, (nl - 4) // 2)]
                        if en == "p":
                            wait(VE, "v", s_p, val)
                    wait(VE, "v", s_v, m.get(("h0c", ncol), 0))
                    inc(VE.tensor_tensor_scan(
                        h_t[:, hslot * TT:(hslot + 1) * TT],
                        dA_t[:, aslot * TT:(aslot + 1) * TT],
                        Bu_t[:, aslot * TT:(aslot + 1) * TT],
                        h0[:, ncol:ncol + 1], ALU.mult, ALU.add), s_v)
                    m[("scan", k, s, nl)] = s_v.n
                    if nl % 2 == 1:
                        wait(VE, "v", s_v, m[("scan", k, s, nl)])
                        inc(VE.tensor_copy(h0[:, ncol - 1:ncol],
                                           h_t[:, (hslot - 1) * TT + TT - 1:
                                               (hslot - 1) * TT + TT]), s_v)
                        inc(VE.tensor_copy(h0[:, ncol:ncol + 1],
                                           h_t[:, hslot * TT + TT - 1:
                                               hslot * TT + TT]), s_v)
                        m[("h0c", ncol - 1)] = s_v.n - 1
                        m[("h0c", ncol)] = s_v.n

                    if nl in (3, 5, 7) and (nl - 3) // 2 < len(UPS_SCHED[s]):
                        emit_relu(k, UPS_SCHED[s][(nl - 3) // 2])
                    if nl in (3, 5, 7):
                        emit_wmul(k, s, (nl - 3) // 2)
                    if nl == 4:
                        emit_adds(k, s, [0])
                    if nl == 6:
                        emit_adds(k, s, [1])

                m[("h0done", k)] = s_v.n

                if 14 <= s <= 17 and k + 1 < NTT:
                    emit_stats(k + 1, s - 14)

                prep_pump(6)

            emit_wmul(k, 31, 3)
            emit_adds(k, 31, [2, 3])
            for (q, ddc) in DNS_SCHED[31]:
                emit_dne(k, q, ddc)

        emit_epilogue(NTT - 1)

        prep_pump_until(PREP_TOTAL)
        SY.wait_ge(s_st[0].h, 16 * stcnt[0])
        SY.wait_ge(s_st[1].h, 16 * stcnt[1])
    return nc


def _pack_inputs(W_xproj, W_dt, b_dt, A_log, D_param, up_w, dn_w, us, ds, osf):
    f32 = np.float32
    wx = np.ascontiguousarray(
        np.asarray(W_xproj, f32).T.reshape(16, 128, 160)
        .transpose(1, 0, 2).reshape(128, 16 * 160))
    wdt = np.ascontiguousarray(np.asarray(W_dt, f32).T)
    alog = np.ascontiguousarray(
        np.asarray(A_log, f32).reshape(16, 128, 16)
        .transpose(1, 0, 2).reshape(128, 256))
    bdt = np.ascontiguousarray(np.asarray(b_dt, f32).reshape(16, 128).T)
    dpar = np.ascontiguousarray(np.asarray(D_param, f32).reshape(16, 128).T)
    usc = np.ascontiguousarray(np.asarray(us, f32).reshape(64, 128).T)
    dsc = np.ascontiguousarray(np.asarray(ds, f32).reshape(16, 128).T)
    osc = np.ascontiguousarray(
        np.broadcast_to(np.asarray(osf, f32).reshape(1, 1), (128, 1)))
    upw = np.ascontiguousarray(
        np.asarray(up_w, f32).reshape(64, 128, 16, 128)
        .transpose(0, 3, 2, 1).reshape(64, 128, 2048))
    dnw = np.ascontiguousarray(
        np.asarray(dn_w, f32).reshape(16, 128, 2, 32, 128)
        .transpose(2, 0, 4, 3, 1).reshape(32, 128, 4096))
    identf = np.eye(128, dtype=f32)
    identb = np.eye(128, dtype=ml_dtypes.bfloat16)
    cvals = np.ascontiguousarray(
        np.broadcast_to(np.array([[1e-5, -0.5, -1.0]], f32), (128, 3)))
    sels = []
    for nh in range(2):
        sel = np.zeros((32, 16, 128), dtype=ml_dtypes.bfloat16)
        for i in range(16):
            row = (nh * 8 + i) if i < 8 else (16 + nh * 8 + (i - 8))
            sel[row, i, :] = 1.0
        sels.append(np.ascontiguousarray(sel.reshape(32, 2048)))
    return dict(wx=wx, wdt=wdt, alog=alog, bdt=bdt, dpar=dpar, usc=usc,
                dsc=dsc, osc=osc, upw=upw, dnw=dnw, identf=identf,
                identb=identb, sel0=sels[0], sel1=sels[1], cvals=cvals)


def kernel(L0_attn_x, L0_attn_W_xproj, L0_attn_W_dt, L0_attn_b_dt,
           L0_attn_A_log, L0_attn_D_param, L0_ffn_up_weight,
           L0_ffn_down_weight, L0_ffn_up_scale, L0_ffn_down_scale,
           L0_ffn_output_scale):
    shared = _pack_inputs(L0_attn_W_xproj, L0_attn_W_dt, L0_attn_b_dt,
                          L0_attn_A_log, L0_attn_D_param, L0_ffn_up_weight,
                          L0_ffn_down_weight, L0_ffn_up_scale,
                          L0_ffn_down_scale, L0_ffn_output_scale)
    x = np.asarray(L0_attn_x, np.float32)
    nc = build_nc()
    in_maps = []
    for c in range(8):
        im = dict(shared)
        im["x"] = np.ascontiguousarray(x[2 * c:2 * c + 2].reshape(B_CORE * T, D))
        in_maps.append(im)
    res = run_bass_kernel_spmd(nc, in_maps, core_ids=list(range(8)))
    out = np.stack([np.asarray(r["out"], np.float32).reshape(B_CORE, T, D)
                    for r in res.results], axis=0)
    return np.ascontiguousarray(out.reshape(16, T, D))
